# version 21 (final)
# baseline (speedup 1.0000x reference)
import math
import numpy as np
import ml_dtypes
import concourse.bass as bass
import concourse.mybir as mybir
from concourse.bass_utils import run_bass_kernel_spmd

F32 = mybir.dt.float32
BF16 = mybir.dt.bfloat16
I32 = mybir.dt.int32
AF = mybir.ActivationFunctionType
ALU = mybir.AluOpType
AXX = mybir.AxisListType.X

NCORE = 8
TPC = 4096
D = 2048
H = 4
DKH = 512
DVH = 256
VD = 1024
FF = 8192
TA = 512
NTA = TPC // TA
CH = 256
NCH = TPC // CH
TB = 512
NTB = TPC // TB
EPS = 1e-5
LNS = float(-0.5 * math.log(512.0))
GC1 = float(2.0 * math.sqrt(2.0 / math.pi))
GC2 = float(2.0 * math.sqrt(2.0 / math.pi) * 0.044715)
MAPBASE = 16512
SBUF_TOP = 229344
DEBUG_TAPS = False


DMA_SLOTS = 8


class Plan:
    def __init__(self):
        self.steps = []
        self.lastw = {}
        self.readers = {}
        self.dmaq = {}

    def emit(self, eng, fn, reads=(), writes=(), kind="c", tag=""):
        deps = set(self.fence_deps)
        for r in reads:
            w = self.lastw.get(r)
            if w is not None:
                deps.add(w)
        for w_ in writes:
            w = self.lastw.get(w_)
            if w is not None:
                deps.add(w)
            for rd in self.readers.get(w_, ()):
                deps.add(rd)
        i = len(self.steps)
        if kind == "d":
            q = self.dmaq.setdefault(eng, [])
            if len(q) >= DMA_SLOTS:
                deps.add(q[-DMA_SLOTS])
            q.append(i)
        self.steps.append(dict(i=i, eng=eng, fn=fn, deps=sorted(deps),
                               kind=kind, tag=tag, sem=None, val=None, inc=0))
        for r in reads:
            self.readers.setdefault(r, []).append(i)
        for w_ in writes:
            self.lastw[w_] = i
            self.readers[w_] = []
        return i

    def c(self, eng, fn, reads=(), writes=(), tag=""):
        return self.emit(eng, fn, reads, writes, "c", tag)

    def d(self, eng, fn, reads=(), writes=(), tag=""):
        return self.emit(eng, fn, reads, writes, "d", tag)

    def cc(self, fn, reads=(), writes=(), tag=""):
        return self.emit("gpsimd", fn, reads, writes, "cc", tag)

    def fence(self):
        deps = set()
        last = {}
        for s in self.steps:
            if s["kind"] != "d":
                last[(s["eng"], s["kind"])] = s["i"]
        deps.update(last.values())
        for q in self.dmaq.values():
            deps.update(q[-DMA_SLOTS:])
        self.fence_deps = sorted(deps)

    fence_deps = ()

    def finalize(self):
        dmaidx = {}
        for s in self.steps:
            if s["kind"] == "d":
                qi = dmaidx.get(s["eng"], 0)
                dmaidx[s["eng"]] = qi + 1
                s["klass"] = ("dma", s["eng"], qi % DMA_SLOTS)
            elif s["kind"] == "cc":
                s["klass"] = ("cc",)
            else:
                s["klass"] = ("c", s["eng"])
        need = set()
        for s in self.steps:
            grouped = {}
            for di in s["deps"]:
                ds = self.steps[di]
                if (ds["eng"] == "tensor" and s["eng"] == "tensor"
                        and ds["kind"] == "c" and s["kind"] == "c"):
                    continue
                k = ds["klass"]
                if grouped.get(k, -1) < di:
                    grouped[k] = di
            s["wdeps"] = sorted(grouped.values())
            need.update(s["wdeps"])
        ctr = {}
        for s in self.steps:
            if s["kind"] == "d":
                key = s["klass"]
                ctr[key] = ctr.get(key, 0) + 16
                s["sem"], s["val"], s["inc"] = key, ctr[key], 16
            elif s["kind"] == "cc":
                key = s["klass"]
                ctr[key] = ctr.get(key, 0) + 1
                s["sem"], s["val"], s["inc"] = key, ctr[key], 1
            elif s["i"] in need:
                key = s["klass"]
                ctr[key] = ctr.get(key, 0) + 1
                s["sem"], s["val"], s["inc"] = key, ctr[key], 1
        self.final_counts = dict(ctr)

    def replay(self, eng_name, eng, sems, hw):
        for s in self.steps:
            if s["eng"] != eng_name:
                continue
            wants = {}
            for di in s["wdeps"]:
                ds = self.steps[di]
                if ds["sem"] is None:
                    continue
                k, v = ds["sem"], ds["val"]
                if wants.get(k, 0) < v:
                    wants[k] = v
            for k in sorted(wants):
                v = wants[k]
                if hw.get((eng_name, k), 0) >= v:
                    continue
                eng.wait_ge(sems[k], v)
                hw[(eng_name, k)] = v
            inst = s["fn"](eng)
            if s["inc"]:
                inst.then_inc(sems[s["sem"]], s["inc"])


def build_program():
    import contextlib
    nc = bass.Bass("TRN2", target_bir_lowering=False, debug=False)
    stack = contextlib.ExitStack()

    def inp(name, shape, dtype=F32):
        return nc.dram_tensor(name, list(shape), dtype, kind="ExternalInput").ap()

    x_in = inp("x", [TPC, D])
    wq_in = inp("wq", [D, D])
    wk_in = inp("wk", [D, D])
    wv_in = inp("wv", [D, VD])
    wgk1_in = inp("wgk1", [D, 16])
    wgk2_in = inp("wgk2", [16, D])
    bgk2_in = inp("bgk2f", [128, 16])
    wg_in = inp("wg", [D, VD])
    wnormbc_in = inp("wnormbc", [128, VD])
    wo_in = inp("wo", [VD, D])
    wupsl_in = inp("wupsl", [256, FF])
    ubf_in = inp("ubf", [128, 64])
    wdnsl_in = inp("wdnsl", [1024, D])
    dbbc_in = inp("dbbc", [128, D])
    pmask_in = inp("pmask", [128, NCORE])
    idf_in = inp("idf", [128, 128])
    idb_in = inp("idb", [128, 128], BF16)
    maskm_in = inp("maskm", [128, CH])
    cpool_in = inp("cpool", [128, 8])

    out_hbm = nc.dram_tensor("out", [TPC, D], F32, kind="ExternalOutput").ap()

    scr_xn = nc.dram_tensor("scr_xn", [NTA, 128, 16 * TA], BF16).ap()
    scr_qe = nc.dram_tensor("scr_qe", [H, NTA, 128, 4 * TA], BF16).ap()
    scr_o = nc.dram_tensor("scr_o", [NTA, 4, 128, VD], F32).ap()
    scr_g = nc.dram_tensor("scr_g", [NTA, 4, 128, VD], F32).ap()
    scr_y = nc.dram_tensor("scr_y", [NTB, 4, 128, D], F32).ap()
    scr_wo = nc.dram_tensor("scr_wo", [VD, D], BF16).ap()
    scr_wq = nc.dram_tensor("scr_wq", [D, D], BF16).ap()
    scr_wk = nc.dram_tensor("scr_wk", [D, D], BF16).ap()
    scr_wv = nc.dram_tensor("scr_wv", [D, VD], BF16).ap()
    scr_wg = nc.dram_tensor("scr_wg", [D, VD], BF16).ap()
    wup_loc = nc.dram_tensor("wup_loc", [2 * 16 * 128, 512], BF16).ap()
    wup_all = nc.dram_tensor("wup_all", [16 * 16 * 128, 512], BF16).ap()
    wdn_loc = nc.dram_tensor("wdn_loc", [8 * 2 * 128, 1024], BF16).ap()
    wdn_all = nc.dram_tensor("wdn_all", [64 * 2 * 128, 1024], BF16).ap()
    st_loc = nc.dram_tensor("st_loc", [H * 4 * 128, DVH], F32).ap()
    st_all_h = [nc.dram_tensor(f"st_all{h}", [NCORE * 4 * 128, DVH], F32).ap()
                for h in range(H)]

    _off = [MAPBASE]

    def sb(name, shape, dtype, at=None):
        esz = {F32: 4, BF16: 2, I32: 4}[dtype]
        nbytes = int(np.prod(shape[1:])) * esz
        if at is None:
            at = _off[0]
            _off[0] = at + ((nbytes + 31) // 32) * 32
        assert at + nbytes <= SBUF_TOP, (name, at, nbytes)
        return nc.alloc_sbuf_tensor_at(name, list(shape), dtype, offset=at).ap()

    cpool = sb("cpool", [128, 8], F32)
    idf = sb("idf", [128, 128], F32)
    idb = sb("idb", [128, 128], BF16)
    maskm = sb("maskm", [128, CH], F32)
    wnormbc = sb("wnormbc", [128, VD], F32)
    bgk2f = sb("bgk2f", [128, 16], F32)
    nbgk2f = sb("nbgk2f", [128, 16], F32)
    pmask = sb("pmask", [128, NCORE], F32)
    btot = sb("btot", [128, H * 4 * NCH], F32)
    eblB = sb("eblB", [128, H * 4 * NCH], F32)
    eblA = sb("eblA", [128, 8], F32)
    eblA2 = sb("eblA2", [128, 8], F32)
    ubf = sb("ubf", [128, 64], F32)
    dbbc = sb("dbbc", [128, D], F32)
    zeros = sb("zeros", [128, CH], F32)
    NSET = 6
    statS = [sb(f"statb{i}", [128, 48], F32) for i in range(NSET)]
    mvS = [sb(f"mvb{i}", [128, 2], F32) for i in range(NSET)]
    rS = [sb(f"rcol{i}", [128, 1], F32) for i in range(NSET)]
    bS = [sb(f"bcol{i}", [128, 1], F32) for i in range(NSET)]
    t1S = [sb(f"lnt1{i}", [128, 1], F32) for i in range(NSET)]
    rmsEs = [sb(f"rmsE{i}", [128, 8], F32) for i in range(2)]
    rmsRs = [sb(f"rmsR{i}", [128, 8], F32) for i in range(2)]
    ARENA = _off[0]

    _off[0] = ARENA
    wqs = sb("wqs", [128, 16 * DKH], BF16)
    wks = sb("wks", [128, 16 * DKH], BF16)
    wvs = sb("wvs", [128, 16 * DVH], BF16)
    wgs = sb("wgs", [128, 16 * DVH], BF16)
    wgk2s = sb("wgk2s", [16, DKH], BF16)
    wgk1s = sb("wgk1s", [128, 16 * 16], BF16)
    z1f = sb("z1f", [16, TPC], BF16)
    xnf_b0 = sb("xnf_b0", [128, 16 * TA], BF16)
    xnf_b1 = sb("xnf_b1", [128, 16 * TA], BF16)
    gkf = sb("gkf", [128, 4 * TA], F32)
    bpr = sb("bpr", [128, 4 * TA], F32)
    enb = sb("enb", [128, 4 * TA], F32)
    qe_b0 = sb("qe_b0", [128, 4 * TA], BF16)
    qe_b1 = sb("qe_b1", [128, 4 * TA], BF16)
    kd_b0 = sb("kd_b0", [128, 4 * TA], BF16)
    kd_b1 = sb("kd_b1", [128, 4 * TA], BF16)
    kdt = sb("kdt", [128, 2 * DKH], BF16)
    vt_b0 = sb("vt_b0", [128, 4 * DVH], BF16)
    vt_b1 = sb("vt_b1", [128, 4 * DVH], BF16)
    gt = sb("gt", [128, 4 * DVH], F32)
    Ab = sb("Ab", [128, 2 * CH], BF16)
    S0 = sb("S0", [128, 4 * DVH], F32)
    S1 = sb("S1", [128, 4 * DVH], F32)
    Sbuf = sb("Sbuf", [128, 4 * DVH], BF16)
    ot = sb("ot", [128, 4 * DVH], F32)
    XLD_OFF = _off[0]
    xld = sb("xld", [128, 2 * D], F32)
    xnt = sb("xnt", [128, D], BF16)
    A_END = _off[0]

    _off[0] = ARENA
    hfm = sb("hfm", [128, 64 * TB], BF16)
    YNFM_OFF = _off[0]
    ynfm = sb("ynfm", [128, 16 * TB], BF16)
    obuf = sb("obuf", [128, 4 * VD], F32)
    gbuf = sb("gbuf", [128, 4 * VD], F32)
    qebuf = sb("qebuf", [128, 16 * CH], BF16)
    Sp = sb("Sp", [128, 16 * DVH], F32)
    Spb = sb("Spb", [128, 4 * DVH], BF16)
    MIXA_OFF = _off[0]
    oftm = sb("oftm", [128, 2 * VD], BF16)
    MIXB_OFF = _off[0]
    yntm = sb("yntm", [128, D], BF16)
    offm = sb("offm", [128, 8 * TB], BF16)
    xyb = sb("xyb", [128, D], F32)
    WOB_OFF = _off[0]
    wob = sb("wob", [128, 2 * D], BF16)
    upb = sb("upb", [128, 2 * 4 * 512], BF16)
    dnb = sb("dnb", [128, 2 * 2 * 1024], BF16)
    B_END = _off[0]
    assert max(A_END, B_END) <= SBUF_TOP - 64, (A_END, B_END)
    glu_s = nc.alloc_sbuf_tensor_at("glu_s", [128, 512], F32,
                                    offset=MIXA_OFF).ap()
    glu_u = nc.alloc_sbuf_tensor_at("glu_u", [128, 512], F32,
                                    offset=MIXA_OFF + 2048).ap()
    glu_z = nc.alloc_sbuf_tensor_at("glu_z", [128, 512], F32,
                                    offset=MIXB_OFF).ap()
    ffn0 = nc.alloc_sbuf_tensor_at("ffn0", [128, 4 * 1024], F32,
                                   offset=MIXA_OFF).ap()
    ffn1 = nc.alloc_sbuf_tensor_at("ffn1", [128, 1024], F32,
                                   offset=WOB_OFF).ap()
    ffn04 = nc.alloc_sbuf_tensor_at("ffn04", [128, 4 * VD], F32,
                                    offset=YNFM_OFF).ap()
    ffn14 = nc.alloc_sbuf_tensor_at("ffn14", [128, 4 * VD], F32,
                                    offset=ARENA).ap()
    xyb4 = nc.alloc_sbuf_tensor_at("xyb4", [128, 4 * D], F32,
                                   offset=ARENA + 16384).ap()

    st_tmp = nc.alloc_sbuf_tensor_at("st_tmp", [128, 4 * DVH], F32,
                                     offset=ARENA).ap()
    st_tmp2 = nc.alloc_sbuf_tensor_at("st_tmp2", [128, 4 * DVH], F32,
                                      offset=YNFM_OFF).ap()
    _off[0] = A_END
    dq_row = nc.alloc_sbuf_tensor_at("dq_row", [128, 4096], F32,
                                     offset=XLD_OFF).ap()
    dq_wt = sb("dq_wt", [128, 512], F32)
    dq_am = sb("dq_am", [128, 16], F32)
    dq_bi = sb("dq_bi", [128, 16], I32)
    dq_sc = sb("dq_sc", [128, 16], F32)
    dq_iv = sb("dq_iv", [128, 16], F32)
    dq_x = sb("dq_x", [128, 512], F32)
    dq_a = sb("dq_a", [128, 512], F32)
    dq_f = sb("dq_f", [128, 512], F32)
    dq_c = sb("dq_c", [128, 512], F32)
    dq_q = sb("dq_q", [128, 512], BF16)
    dq_st = sb("dq_st", [128, 512], BF16)
    A_END2 = _off[0]
    assert A_END2 <= SBUF_TOP - 64, (A_END, B_END, A_END2)
    wo_cst = nc.alloc_sbuf_tensor_at("wo_cst", [128, D], BF16,
                                     offset=XLD_OFF).ap()

    ps = [stack.enter_context(nc.psum_tensor(f"psum{i}", [128, 512], F32))
          for i in range(8)]
    psb = [p.ap().bitcast(BF16) for p in ps]
    psr = [f"ps{i}" for i in range(8)]
    ps = [p.ap() for p in ps]

    sems = {}
    semlist = [(("c", "tensor"), "s_pe"), (("c", "vector"), "s_dv"),
               (("c", "scalar"), "s_ac"), (("c", "gpsimd"), "s_gp"),
               (("cc",), "s_cc")]
    for q in ("sync", "scalar", "gpsimd"):
        for j in range(DMA_SLOTS):
            semlist.append((("dma", q, j), f"d_{q[:2]}{j}"))
    for key, name in semlist:
        sems[key] = stack.enter_context(nc.semaphore(name))

    CPOOL_VALS = [0.0, 1.0, EPS, LNS, -1.0, -0.5, 1.0 / 16.0, -1.0 / 16.0]
    for i, v in enumerate(CPOOL_VALS):
        nc.const_aps.register(F32, v, cpool[:, i:i + 1])

    P = Plan()

    dbg = {}
    if DEBUG_TAPS:
        for nm, shp, dty in [("d_sil", [128, 2 * VD], F32),
                             ("d_oftm", [2, 128, 2 * VD], F32),
                             ("d_offm", [128, 8 * TB], F32),
                             ("d_anb", [128, D], F32),
                             ("d_rmsR", [2, 128, 8], F32),
                             ("d_Sp", [128, 16 * DVH], F32),
                             ("d_ynfm", [128, 16 * TB], F32),
                             ("d_hfm", [128, 64 * TB], F32)]:
            dbg[nm] = nc.dram_tensor(nm, shp, dty, kind="ExternalOutput").ap()

    def tap(nm, ap, res, idx=None):
        if not DEBUG_TAPS:
            return
        dst = dbg[nm] if idx is None else dbg[nm][idx]
        P.d("gpsimd", lambda e, dst=dst, ap=ap: e.dma_start(out=dst, in_=ap),
            reads=(res,), writes=())

    def v3(ap, n, m):
        return ap.rearrange("p (a b) -> p a b", b=m)

    xnfs = [xnf_b0, xnf_b1]
    xnf3s = [v3(xnf_b0, 16, TA), v3(xnf_b1, 16, TA)]
    wqs3 = v3(wqs, 16, DKH)
    wks3 = v3(wks, 16, DKH)
    wvs3 = v3(wvs, 16, DVH)
    wgs3 = v3(wgs, 16, DVH)
    wgk1s3 = v3(wgk1s, 16, 16)
    gkf3 = v3(gkf, 4, TA)
    bpr3 = v3(bpr, 4, TA)
    enb3 = v3(enb, 4, TA)
    qes = [qe_b0, qe_b1]
    qe3s = [v3(qe_b0, 4, TA), v3(qe_b1, 4, TA)]
    kd3s = [v3(kd_b0, 4, TA), v3(kd_b1, 4, TA)]
    kdt3 = v3(kdt, 2, DKH)
    vt3s = [v3(vt_b0, 4, DVH), v3(vt_b1, 4, DVH)]
    eblAs = [eblA, eblA2]
    gt3 = v3(gt, 4, DVH)
    Ab3 = v3(Ab, 2, CH)
    S03 = v3(S0, 4, DVH)
    S13 = v3(S1, 4, DVH)
    Sb3 = v3(Sbuf, 4, DVH)
    ot3 = v3(ot, 4, DVH)
    xld3 = v3(xld, 2, D)
    hfm3 = v3(hfm, 64, TB)
    ynfm3 = v3(ynfm, 16, TB)
    ob3 = v3(obuf, 4, VD)
    gb3 = v3(gbuf, 4, VD)
    qeb3 = v3(qebuf, 16, CH)
    Sp3 = v3(Sp, 16, DVH)
    Spb3 = v3(Spb, 4, DVH)
    oftm3 = v3(oftm, 2, VD)
    offm3 = v3(offm, 8, TB)
    wob3 = v3(wob, 2, D)
    upb4 = upb.rearrange("p (s k f) -> p s k f", s=2, f=512)
    dnb4 = dnb.rearrange("p (s k f) -> p s k f", s=2, f=1024)
    ffn03 = v3(ffn0, 4, 1024)
    ffn043 = v3(ffn04, 4, VD)
    ffn143 = v3(ffn14, 4, VD)
    xyb43 = v3(xyb4, 4, D)
    st_tmp3 = v3(st_tmp, 4, DVH)
    st_tmp23 = v3(st_tmp2, 4, DVH)

    def ln_stats(src_fn, nchunks, rtag, si=0):
        sres, rres = f"stat{si}", f"rb{si}"
        sB, mB, rB, bB, tB = statS[si], mvS[si], rS[si], bS[si], t1S[si]
        for i in range(nchunks):
            P.c("vector", lambda e, i=i, sB=sB: e.bn_stats(
                sB[:, 6 * i:6 * (i + 1)], src_fn(i)),
                reads=rtag, writes=(sres,))
        P.c("vector", lambda e, n=nchunks, sB=sB, mB=mB: e.bn_aggr(
            mB[:, :], sB[:, 0:6 * n]),
            reads=(sres,), writes=(sres,))
        P.c("scalar", lambda e, mB=mB, tB=tB: e.activation(
            tB[:, :], mB[:, 1:2], AF.Ln, bias=EPS, scale=1.0),
            reads=(sres,), writes=(rres,))
        P.c("scalar", lambda e, tB=tB, rB=rB: e.activation(
            rB[:, :], tB[:, :], AF.Exp, bias=0.0, scale=-0.5),
            reads=(rres,), writes=(rres,))
        P.c("vector", lambda e, mB=mB, rB=rB, bB=bB: e.tensor_scalar(
            bB[:, :], mB[:, 0:1], rB[:, :], -1.0, ALU.mult, ALU.mult),
            reads=(sres, rres), writes=(rres,))
        return rB, bB, rres

    XLD_RES0 = ("xld0", "xld1")
    P.d("sync", lambda e: e.dma_start(out=cpool[:, :], in_=cpool_in),
        writes=("cpool",))
    P.fence()
    for dst, src, res in [(idf, idf_in, "idf"), (idb, idb_in, "idb"),
                          (maskm, maskm_in, "maskm"),
                          (wnormbc, wnormbc_in, "wnormbc"),
                          (bgk2f, bgk2_in, "bgk2f"), (pmask, pmask_in, "pmask"),
                          (ubf, ubf_in, "ubf"), (dbbc, dbbc_in, "dbbc")]:
        P.d("sync", lambda e, d=dst, s=src: e.dma_start(out=d[:, :], in_=s),
            writes=(res,))
    P.c("vector", lambda e: e.memset(zeros[:, :], 0.0), writes=("zeros",))
    P.c("vector", lambda e: e.tensor_scalar(
        nbgk2f[:, :], bgk2f[:, :], -1.0, None, ALU.mult),
        reads=("bgk2f",), writes=("nbgk2f",))
    def precast(w_in_, scr, nrow, ncol, res):
        for rr in range(nrow // 128):
            P.d("gpsimd", lambda e, rr=rr, w_in_=w_in_, ncol=ncol: e.dma_start(
                out=wo_cst[:, 0:ncol],
                in_=w_in_[128 * rr:128 * (rr + 1), :]),
                reads=XLD_RES0, writes=XLD_RES0)
            P.d("sync", lambda e, rr=rr, scr=scr, ncol=ncol: e.dma_start(
                out=scr[128 * rr:128 * (rr + 1), :], in_=wo_cst[:, 0:ncol]),
                reads=XLD_RES0, writes=(res,))

    precast(wo_in, scr_wo, VD, D, "scr_wo")
    precast(wq_in, scr_wq, D, D, "scr_wq")
    precast(wk_in, scr_wk, D, D, "scr_wk")
    precast(wv_in, scr_wv, D, VD, "scr_wv")
    precast(wg_in, scr_wg, D, VD, "scr_wg")

    XLD_RES = ("xld0", "xld1")

    def dq_load(w_sl, rc, c0, c1, d0):
        def th():
            P.d("sync", lambda e: e.dma_start(
                out=dq_row[:, d0:d0 + (c1 - c0)],
                in_=w_sl[128 * rc:128 * (rc + 1), c0:c1]),
                reads=XLD_RES, writes=XLD_RES)
        return th

    def dq_group_a(goff):
        def th():
            for i in range(4):
                P.c("tensor", lambda e, i=i: e.transpose(
                    ps[2][:, 128 * i:128 * (i + 1)],
                    dq_row[:, 512 * goff + 128 * i:512 * goff + 128 * (i + 1)],
                    idf[:, :]),
                    reads=XLD_RES + ("idf",), writes=(psr[2],))
            P.c("vector", lambda e: e.tensor_copy(dq_wt[:, :], ps[2][:, :]),
                reads=(psr[2],), writes=("dq_wt",))
            P.c("vector", lambda e: e.tensor_reduce(
                dq_am[:, :], dq_wt.rearrange("p (b k) -> p b k", k=32),
                axis=AXX, op=ALU.max, apply_absolute_value=True),
                reads=("dq_wt",), writes=("dq_am",))
            P.c("vector", lambda e: e.tensor_scalar(
                dq_am[:, :], dq_am[:, :], 1e-30, None, ALU.max),
                reads=("dq_am",), writes=("dq_am",))
            P.c("vector", lambda e: e.tensor_scalar(
                dq_bi[:, :], dq_am[:, :].bitcast(I32), 0x7F800000, None,
                ALU.bitwise_and),
                reads=("dq_am",), writes=("dq_bi",))
            P.c("vector", lambda e: e.tensor_scalar(
                dq_sc[:, :], dq_bi[:, :].bitcast(F32), 0.25, None, ALU.mult),
                reads=("dq_bi",), writes=("dq_sc",))
            P.c("vector", lambda e: e.tensor_scalar(
                dq_iv[:, :].bitcast(I32), dq_bi[:, :], -1, -0x80000000,
                ALU.mult, ALU.subtract),
                reads=("dq_bi",), writes=("dq_iv",))
            P.c("vector", lambda e: e.tensor_tensor(
                dq_x.rearrange("p (b k) -> p b k", k=32),
                dq_wt.rearrange("p (b k) -> p b k", k=32),
                dq_iv[:, :].unsqueeze(2).broadcast_to([128, 16, 32]),
                ALU.mult),
                reads=("dq_wt", "dq_iv"), writes=("dq_x",))
            P.c("scalar", lambda e: e.activation(dq_a[:, :], dq_x[:, :],
                                                 AF.Abs),
                reads=("dq_x",), writes=("dq_a",))
            P.c("vector", lambda e: e.tensor_scalar(
                dq_f[:, :], dq_a[:, :], 0.25, None, ALU.is_ge),
                reads=("dq_a",), writes=("dq_f",))
            for thr in (0.75, 1.25, 1.75):
                P.c("vector", lambda e, thr=thr: e.scalar_tensor_tensor(
                    dq_f[:, :], dq_a[:, :], thr, dq_f[:, :],
                    ALU.is_ge, ALU.add),
                    reads=("dq_a", "dq_f"), writes=("dq_f",))
            P.c("vector", lambda e: e.tensor_scalar(
                dq_c[:, :], dq_a[:, :], 2.5, None, ALU.is_ge),
                reads=("dq_a",), writes=("dq_c",))
            P.c("vector", lambda e: e.scalar_tensor_tensor(
                dq_f[:, :], dq_f[:, :], 0.5, dq_c[:, :], ALU.mult, ALU.add),
                reads=("dq_f", "dq_c"), writes=("dq_f",))
            P.c("vector", lambda e: e.scalar_tensor_tensor(
                dq_f[:, :], dq_a[:, :], 3.5, dq_f[:, :], ALU.is_ge, ALU.add),
                reads=("dq_a", "dq_f"), writes=("dq_f",))
            P.c("vector", lambda e: e.tensor_scalar(
                dq_c[:, :], dq_a[:, :], 5.0, None, ALU.is_ge),
                reads=("dq_a",), writes=("dq_c",))
            P.c("vector", lambda e: e.scalar_tensor_tensor(
                dq_f[:, :], dq_c[:, :], 2.0, dq_f[:, :], ALU.mult, ALU.add),
                reads=("dq_f", "dq_c"), writes=("dq_f",))
            P.c("scalar", lambda e: e.sign(dq_c[:, :], dq_x[:, :]),
                reads=("dq_x",), writes=("dq_c",))
            P.c("vector", lambda e: e.tensor_tensor(
                dq_f[:, :], dq_f[:, :], dq_c[:, :], ALU.mult),
                reads=("dq_f", "dq_c"), writes=("dq_f",))
            P.c("vector", lambda e: e.tensor_tensor(
                dq_q.rearrange("p (b k) -> p b k", k=32),
                dq_f.rearrange("p (b k) -> p b k", k=32),
                dq_sc[:, :].unsqueeze(2).broadcast_to([128, 16, 32]),
                ALU.mult),
                reads=("dq_f", "dq_sc"), writes=("dq_q",))
        return th

    def dq_group_b(dst_ap):
        def th():
            for i in range(4):
                P.c("tensor", lambda e, i=i: e.transpose(
                    psb[3][:, 128 * i:128 * (i + 1)],
                    dq_q[:, 128 * i:128 * (i + 1)], idb[:, :]),
                    reads=("dq_q", "idb"), writes=(psr[3],))
            P.c("vector", lambda e: e.tensor_copy(
                dq_st[:, :], psb[3][:, 0:512]),
                reads=(psr[3],), writes=("dq_st",))
            P.d("scalar", lambda e: e.dma_start(out=dst_ap, in_=dq_st[:, :]),
                reads=("dq_st",), writes=("wloc",))
        return th

    wup_loc3 = wup_loc.rearrange("(k p) f -> k p f", p=128)
    wdn_loc3 = wdn_loc.rearrange("(k p) f -> k p f", p=128)

    def _ag_wup():
        P.cc(lambda e: e.collective_compute(
            "AllGather", ALU.bypass, replica_groups=[list(range(NCORE))],
            ins=[wup_loc[:, :].opt()], outs=[wup_all[:, :].opt()]),
            reads=("wloc",), writes=("wup_all",), tag="ag_wup")

    def _ag_wdn():
        P.cc(lambda e: e.collective_compute(
            "AllGather", ALU.bypass, replica_groups=[list(range(NCORE))],
            ins=[wdn_loc[:, :].opt()], outs=[wdn_all[:, :].opt()]),
            reads=("wloc",), writes=("wdn_all",), tag="ag_wdn")

    dq_units = []
    _pend = []

    def _push(*ths):
        dq_units.append(_pend[:] + list(ths))
        _pend.clear()

    for rc in range(2):
        for half in range(2):
            base = 4096 * half
            _push(dq_load(wupsl_in, rc, base, base + 2048, 0))
            for g in range(8):
                if g == 2:
                    _pend.append(dq_load(wupsl_in, rc, base + 2048,
                                         base + 4096, 2048))
                _push(dq_group_a(g))
                _pend.append(dq_group_b(
                    wup_loc3[rc * 16 + 8 * half + g, :, :]))
    _push(_ag_wup)
    for rc in range(8):
        _push(dq_load(wdnsl_in, rc, 0, 2048, 0))
        for g in range(4):
            _push(dq_group_a(g))
            _pend.append(dq_group_b(
                wdn_loc3[rc * 2 + g // 2, :, 512 * (g % 2):512 * (g % 2 + 1)]))
    _push(_ag_wdn)

    dq_pos = [0]

    def dq_step():
        if dq_pos[0] < len(dq_units):
            for th in dq_units[dq_pos[0]]:
                th()
            dq_pos[0] += 1

    P.fence()

    P.d("gpsimd", lambda e: e.dma_start(
        out=v3(wgk1s, 16, 16), in_=wgk1_in.rearrange("(c p) f -> p c f", p=128)),
        writes=("wgk1s",))

    for t in range(NTA):
        par = t % 2
        xr = f"xnf{par}"
        xnf3 = xnf3s[par]
        for blk in range(4):
            sl = blk % 2
            P.d("sync", lambda e, t=t, blk=blk, sl=sl: e.dma_start(
                out=xld3[:, sl, :],
                in_=x_in[t * TA + 128 * blk:t * TA + 128 * (blk + 1), :]),
                reads=(f"xld{sl}",), writes=(f"xld{sl}",))
            rB, bB, rres = ln_stats(
                lambda i, sl=sl: xld3[:, sl, 512 * i:512 * (i + 1)],
                4, (f"xld{sl}",), si=blk % 2)
            P.c("scalar", lambda e, sl=sl, rB=rB, bB=bB: e.activation(
                xnt[:, :], xld3[:, sl, :], AF.Identity,
                bias=bB[:, :], scale=rB[:, :]),
                reads=(f"xld{sl}", rres), writes=("xnt",))
            for rnd in range(4):
                bank = 6 + rnd % 2
                for i in range(4):
                    P.c("tensor", lambda e, rnd=rnd, i=i, bank=bank: e.transpose(
                        psb[bank][:, 128 * i:128 * (i + 1)],
                        xnt[:, 512 * rnd + 128 * i:512 * rnd + 128 * (i + 1)],
                        idb[:, :]),
                        reads=("xnt", "idb"), writes=(psr[bank],))
                P.c("vector", lambda e, rnd=rnd, blk=blk, bank=bank,
                    xnf3=xnf3: e.tensor_copy(
                    xnf3[:, 4 * rnd:4 * (rnd + 1), 128 * blk:128 * (blk + 1)],
                    psb[bank][:, 0:512].rearrange("p (b f) -> p b f", f=128)),
                    reads=(psr[bank],), writes=(xr,))
        P.d("scalar", lambda e, t=t, par=par: e.dma_start(
            out=scr_xn[t, :, :], in_=xnfs[par][:, :]),
            reads=(xr,), writes=("scr_xn",))
        for kc in range(16):
            P.c("tensor", lambda e, kc=kc, xnf3=xnf3: e.matmul(
                ps[5][0:16, :], wgk1s3[:, kc, :], xnf3[:, kc, :],
                start=(kc == 0), stop=(kc == 15)),
                reads=("wgk1s", xr), writes=(psr[5],))
        P.c("vector", lambda e, t=t: e.tensor_copy(
            z1f[0:16, TA * t:TA * (t + 1)], ps[5][0:16, :]),
            reads=(psr[5],), writes=("z1f",))

    def load_head_weights(h):
        P.d("gpsimd", lambda e, h=h: e.dma_start(
            out=wgk2s[:, :], in_=wgk2_in[:, DKH * h:DKH * (h + 1)]),
            reads=("wgk2s",), writes=("wgk2s",))
        P.d("sync", lambda e, h=h: e.dma_start(
            out=wqs3[:, :, :],
            in_=scr_wq[:, DKH * h:DKH * (h + 1)]
                .rearrange("(c p) f -> p c f", p=128)),
            reads=("wqs", "scr_wq"), writes=("wqs",))
        P.d("sync", lambda e, h=h: e.dma_start(
            out=wks3[:, :, :],
            in_=scr_wk[:, DKH * h:DKH * (h + 1)]
                .rearrange("(c p) f -> p c f", p=128)),
            reads=("wks", "scr_wk"), writes=("wks",))
        P.d("sync", lambda e, h=h: e.dma_start(
            out=wvs3[:, :, :],
            in_=scr_wv[:, DVH * h:DVH * (h + 1)]
                .rearrange("(c p) f -> p c f", p=128)),
            reads=("wvs", "scr_wv"), writes=("wvs",))
        P.d("sync", lambda e, h=h: e.dma_start(
            out=wgs3[:, :, :],
            in_=scr_wg[:, DVH * h:DVH * (h + 1)]
                .rearrange("(c p) f -> p c f", p=128)),
            reads=("wgs", "scr_wg"), writes=("wgs",))

    def emit_proj(h, t):
        par = (h * NTA + t) % 2
        xr = f"xnf{par}"
        xnf3 = xnf3s[par]
        qr, kr_, vr, er = f"qe{par}", f"kd{par}", f"vt{par}", f"ebA{par}"
        qe3, kd3, vt3 = qe3s[par], kd3s[par], vt3s[par]
        eblAv = eblAs[par]
        for dsl in range(4):
            bank = dsl
            P.c("tensor", lambda e, h=h, t=t, dsl=dsl, bank=bank: e.matmul(
                ps[bank][:, :], wgk2s[0:16, 128 * dsl:128 * (dsl + 1)],
                z1f[0:16, TA * t:TA * (t + 1)], start=True, stop=True),
                reads=("wgk2s", "z1f"), writes=(psr[bank],))
            hd = h * 4 + dsl
            P.c("scalar", lambda e, dsl=dsl, hd=hd, bank=bank: e.activation(
                gkf3[:, dsl, :], ps[bank][:, :], AF.Abs,
                bias=bgk2f[:, hd:hd + 1], scale=1.0),
                reads=(psr[bank], "bgk2f"), writes=("gkf",))
            P.c("scalar", lambda e, dsl=dsl, hd=hd, bank=bank: e.activation(
                bpr3[:, dsl, :], ps[bank][:, :], AF.Relu,
                bias=nbgk2f[:, hd:hd + 1], scale=-1.0),
                reads=(psr[bank], "nbgk2f"), writes=("bpr",))
            P.c("scalar", lambda e, dsl=dsl: e.activation(
                enb3[:, dsl, :], gkf3[:, dsl, :], AF.Exp, bias=0.0,
                scale=-1.0),
                reads=("gkf",), writes=("enb",))
            P.c("scalar", lambda e, dsl=dsl: e.activation(
                gkf3[:, dsl, :], enb3[:, dsl, :], AF.Ln, bias=1.0,
                scale=1.0),
                reads=("enb",), writes=("gkf",))
            P.c("vector", lambda e, dsl=dsl: e.tensor_tensor(
                gkf3[:, dsl, :], gkf3[:, dsl, :], bpr3[:, dsl, :], ALU.add),
                reads=("gkf", "bpr"), writes=("gkf",))
        for dsl in range(4):
            for c in range(2):
                P.c("vector", lambda e, dsl=dsl, c=c: e.tensor_tensor_scan(
                    bpr3[:, dsl, CH * c:CH * (c + 1)],
                    gkf3[:, dsl, CH * c:CH * (c + 1)],
                    zeros[:, :], 0.0, ALU.add, ALU.bypass),
                    reads=("gkf", "zeros"), writes=("bpr",))
            col = h * 64 + dsl * 16 + 2 * t
            P.c("vector", lambda e, dsl=dsl, col=col: e.tensor_copy(
                btot[:, col:col + 2],
                bpr3[:, dsl, CH - 1::CH]),
                reads=("bpr",), writes=("btot",))
            P.c("scalar", lambda e, dsl=dsl, col=col, eblAv=eblAv:
                e.activation(
                    eblAv[:, 2 * dsl:2 * dsl + 2], btot[:, col:col + 2],
                    AF.Exp, bias=0.0, scale=-1.0 / 16.0),
                reads=("btot",), writes=(er,))
        for dsl in range(4):
            P.c("scalar", lambda e, dsl=dsl: e.activation(
                enb3[:, dsl, :], bpr3[:, dsl, :], AF.Exp, bias=0.0,
                scale=1.0 / 16.0),
                reads=("bpr",), writes=("enb",))
            P.c("scalar", lambda e, dsl=dsl: e.activation(
                gkf3[:, dsl, :], bpr3[:, dsl, :], AF.Exp, bias=LNS,
                scale=-1.0 / 16.0),
                reads=("bpr",), writes=("gkf",))
        for dsl in range(4):
            bank = dsl
            for kc in range(16):
                P.c("tensor", lambda e, dsl=dsl, kc=kc, bank=bank,
                    xnf3=xnf3: e.matmul(
                    ps[bank][:, :], wqs3[:, kc, 128 * dsl:128 * (dsl + 1)],
                    xnf3[:, kc, :], start=(kc == 0), stop=(kc == 15)),
                    reads=("wqs", xr), writes=(psr[bank],))
            P.c("vector", lambda e, dsl=dsl, bank=bank, qe3=qe3:
                e.tensor_tensor(
                    qe3[:, dsl, :], ps[bank][:, :], gkf3[:, dsl, :], ALU.mult),
                reads=(psr[bank], "gkf"), writes=(qr,))
        dq_step()
        for dsl in range(4):
            bank = dsl
            for kc in range(16):
                P.c("tensor", lambda e, dsl=dsl, kc=kc, bank=bank,
                    xnf3=xnf3: e.matmul(
                    ps[bank][:, :], wks3[:, kc, 128 * dsl:128 * (dsl + 1)],
                    xnf3[:, kc, :], start=(kc == 0), stop=(kc == 15)),
                    reads=("wks", xr), writes=(psr[bank],))
            P.c("vector", lambda e, dsl=dsl, bank=bank, kd3=kd3:
                e.tensor_tensor(
                    kd3[:, dsl, :], ps[bank][:, :], enb3[:, dsl, :], ALU.mult),
                reads=(psr[bank], "enb"), writes=(kr_,))
        dq_step()
        for tb in range(4):
            bank = tb % 2
            for kc in range(16):
                P.c("tensor", lambda e, tb=tb, kc=kc, bank=bank,
                    xnf3=xnf3: e.matmul(
                    ps[bank][:, 0:DVH],
                    xnf3[:, kc, 128 * tb:128 * (tb + 1)],
                    wvs3[:, kc, :], start=(kc == 0), stop=(kc == 15)),
                    reads=(xr, "wvs"), writes=(psr[bank],))
            P.c("scalar", lambda e, tb=tb, bank=bank, vt3=vt3: e.copy(
                vt3[:, tb, :], ps[bank][:, 0:DVH]),
                reads=(psr[bank],), writes=(vr,))
        dq_step()
        for tb in range(4):
            bank = 2 + tb % 2
            for kc in range(16):
                P.c("tensor", lambda e, tb=tb, kc=kc, bank=bank,
                    xnf3=xnf3: e.matmul(
                    ps[bank][:, 0:DVH],
                    xnf3[:, kc, 128 * tb:128 * (tb + 1)],
                    wgs3[:, kc, :], start=(kc == 0), stop=(kc == 15)),
                    reads=(xr, "wgs"), writes=(psr[bank],))
            P.c("scalar", lambda e, tb=tb, bank=bank: e.copy(
                gt3[:, tb, :], ps[bank][:, 0:DVH]),
                reads=(psr[bank],), writes=("gt",))
        P.d("scalar", lambda e, h=h, t=t: e.dma_start(
            out=scr_g[t, :, :, DVH * h:DVH * (h + 1)]
                .rearrange("b p f -> p b f"),
            in_=gt3[:, :, :]),
            reads=("gt",), writes=("scr_g",))
        P.d("scalar", lambda e, h=h, t=t, par=par: e.dma_start(
            out=scr_qe[h, t, :, :], in_=qes[par][:, :]),
            reads=(qr,), writes=("scr_qe",))

    def emit_scan(h, t):
        par = (h * NTA + t) % 2
        qr, kr_, vr, er = f"qe{par}", f"kd{par}", f"vt{par}", f"ebA{par}"
        qe3, kd3, vt3 = qe3s[par], kd3s[par], vt3s[par]
        eblAv = eblAs[par]
        if t == 0:
            P.c("vector", lambda e: e.memset(S0[:, :], 0.0), writes=("S0",))
            P.c("vector", lambda e: e.memset(Sbuf[:, :], 0.0), writes=("Sbuf",))
        for c in range(2):
            gc = t * 2 + c
            Sold = S03 if gc % 2 == 0 else S13
            Snew = S13 if gc % 2 == 0 else S03
            Sold_r = "S0" if gc % 2 == 0 else "S1"
            Snew_r = "S1" if gc % 2 == 0 else "S0"
            for sbk in range(2):
                bank = 4 + sbk
                for dsl in range(4):
                    P.c("tensor", lambda e, c=c, sbk=sbk, dsl=dsl, bank=bank,
                        kd3=kd3: e.transpose(
                            psb[bank][:, 128 * dsl:128 * (dsl + 1)],
                            kd3[:, dsl,
                                CH * c + 128 * sbk:CH * c + 128 * (sbk + 1)],
                            idb[:, :]),
                        reads=(kr_, "idb"), writes=(psr[bank],))
                P.c("vector", lambda e, sbk=sbk, bank=bank: e.tensor_copy(
                    kdt3[:, sbk, :],
                    psb[bank][:, 0:512].rearrange("p (b f) -> p b f", f=128)),
                    reads=(psr[bank],), writes=("kdt",))
            for sbk in range(2):
                bank = 4 + sbk
                ncols = CH if sbk == 0 else 128
                coff = CH * c if sbk == 0 else CH * c + 128
                for dsl in range(4):
                    P.c("tensor",
                        lambda e, c=c, sbk=sbk, dsl=dsl, bank=bank,
                        ncols=ncols, coff=coff, kd3=kd3, qe3=qe3: e.matmul(
                            ps[bank][:, 0:ncols],
                            kd3[:, dsl,
                                CH * c + 128 * sbk:CH * c + 128 * (sbk + 1)],
                            qe3[:, dsl, coff:coff + ncols],
                            start=(dsl == 0), stop=(dsl == 3)),
                        reads=(kr_, qr), writes=(psr[bank],))
            P.c("vector", lambda e: e.tensor_tensor(
                Ab3[:, 0, :], ps[4][:, 0:CH], maskm[:, :], ALU.mult),
                reads=(psr[4], "maskm"), writes=("Ab",))
            P.c("vector", lambda e: e.tensor_tensor(
                Ab3[:, 1, 128:CH], ps[5][:, 0:128], maskm[:, 0:128],
                ALU.mult),
                reads=(psr[5], "maskm"), writes=("Ab",))
            for cb in range(2):
                bank = 6 + cb
                mm = []
                if cb == 0:
                    mm.append((0, 0, 2 * c))
                else:
                    mm.append((0, 128, 2 * c))
                    mm.append((1, 128, 2 * c + 1))
                nmm = len(mm) + (0 if gc == 0 else 4)
                idx = 0
                for (sbk, aoff, vslot) in mm:
                    P.c("tensor",
                        lambda e, sbk=sbk, aoff=aoff, vslot=vslot,
                        bank=bank, idx=idx, nmm=nmm, vt3=vt3: e.matmul(
                            ps[bank][:, 0:DVH],
                            Ab3[:, sbk, aoff:aoff + 128],
                            vt3[:, vslot, :],
                            start=(idx == 0), stop=(idx == nmm - 1)),
                        reads=("Ab", vr), writes=(psr[bank],))
                    idx += 1
                if gc != 0:
                    for dsl in range(4):
                        P.c("tensor",
                            lambda e, c=c, cb=cb, dsl=dsl, bank=bank,
                            idx=idx, nmm=nmm, qe3=qe3: e.matmul(
                                ps[bank][:, 0:DVH],
                                qe3[:, dsl,
                                    CH * c + 128 * cb:CH * c + 128 * (cb + 1)],
                                Sb3[:, dsl, :],
                                start=(idx == 0), stop=(idx == nmm - 1)),
                            reads=(qr, "Sbuf"), writes=(psr[bank],))
                        idx += 1
                P.c("scalar", lambda e, c=c, cb=cb, bank=bank: e.copy(
                    ot3[:, 2 * c + cb, :], ps[bank][:, 0:DVH]),
                    reads=(psr[bank],), writes=("ot",))
            for dsl in range(4):
                bank = 4 + dsl % 2
                for sbk in range(2):
                    P.c("tensor",
                        lambda e, dsl=dsl, sbk=sbk, bank=bank, c=c,
                        vt3=vt3: e.matmul(
                            ps[bank][:, 0:DVH],
                            kdt3[:, sbk, 128 * dsl:128 * (dsl + 1)],
                            vt3[:, 2 * c + sbk, :],
                            start=(sbk == 0), stop=(sbk == 1)),
                        reads=("kdt", vr), writes=(psr[bank],))
                P.c("vector", lambda e, dsl=dsl, bank=bank, Snew=Snew,
                    Sold=Sold: e.tensor_tensor(
                        Snew[:, dsl, :], Sold[:, dsl, :], ps[bank][:, 0:DVH],
                        ALU.add),
                    reads=(Sold_r, psr[bank]), writes=(Snew_r,))
                P.c("vector", lambda e, dsl=dsl, Snew=Snew, c=c,
                    eblAv=eblAv: e.tensor_scalar(
                        Snew[:, dsl, :], Snew[:, dsl, :],
                        eblAv[:, 2 * dsl + c:2 * dsl + c + 1], None, ALU.mult),
                    reads=(Snew_r, er), writes=(Snew_r,))
                P.c("vector", lambda e, dsl=dsl, Snew=Snew: e.tensor_copy(
                    Sb3[:, dsl, :], Snew[:, dsl, :]),
                    reads=(Snew_r,), writes=("Sbuf",))
        P.d("scalar", lambda e, h=h, t=t: e.dma_start(
            out=scr_o[t, :, :, DVH * h:DVH * (h + 1)]
                .rearrange("b p f -> p b f"),
            in_=ot3[:, :, :]),
            reads=("ot",), writes=("scr_o",))
        if t == NTA - 1:
            Sfin = S03 if (NCH - 1) % 2 == 1 else S13
            Sfin_r = "S0" if (NCH - 1) % 2 == 1 else "S1"
            P.d("scalar", lambda e, h=h, Sfin=Sfin: e.dma_start(
                out=st_loc[512 * h:512 * (h + 1), :]
                    .rearrange("(d p) f -> p d f", p=128),
                in_=Sfin[:, :, :]),
                reads=(Sfin_r,), writes=("st_loc",))
            P.cc(lambda e, h=h: e.collective_compute(
                "AllGather", ALU.bypass, replica_groups=[list(range(NCORE))],
                ins=[st_loc[512 * h:512 * (h + 1), :].opt()],
                outs=[st_all_h[h][:, :].opt()]),
                reads=("st_loc",), writes=(f"st_all{h}",), tag=f"ag_st{h}")

    items = [(h, t) for h in range(H) for t in range(NTA)]

    def emit_xnf_load(i):
        t = items[i][1]
        par = i % 2
        xr = f"xnf{par}"
        P.d("sync", lambda e, t=t, par=par: e.dma_start(
            out=xnfs[par][:, :], in_=scr_xn[t, :, :]),
            reads=(xr, "scr_xn"), writes=(xr,))

    dq_step()
    emit_xnf_load(0)
    for i, (h, t) in enumerate(items):
        if t == 0:
            load_head_weights(h)
        if i + 1 < len(items):
            emit_xnf_load(i + 1)
        emit_proj(h, t)
        if i > 0:
            emit_scan(*items[i - 1])
    emit_scan(*items[-1])
    while dq_pos[0] < len(dq_units):
        dq_step()

    P.fence()

    P.c("vector", lambda e: e.memset(Sp[:, :], 0.0),
        writes=("Sp0", "Sp1", "Sp2", "Sp3"))
    for h in range(H):
        for r in range(NCORE):
            sl = (h * NCORE + r) % 2
            stb = st_tmp3 if sl == 0 else st_tmp23
            sres = "hfm" if sl == 0 else "ynfm"
            P.d("sync", lambda e, h=h, r=r, stb=stb: e.dma_start(
                out=stb[:, :, :],
                in_=st_all_h[h][512 * r:512 * (r + 1), :]
                    .rearrange("(g p) f -> p g f", p=128)),
                reads=(f"st_all{h}", sres), writes=(sres,))
            P.c("vector", lambda e, h=h, r=r, stb=stb: e.scalar_tensor_tensor(
                Sp3[:, 4 * h:4 * (h + 1), :], stb[:, :, :],
                pmask[:, r:r + 1], Sp3[:, 4 * h:4 * (h + 1), :],
                ALU.mult, ALU.add),
                reads=(sres, "pmask", f"Sp{h}"), writes=(f"Sp{h}",))
    P.c("scalar", lambda e: e.activation(
        eblB[:, :], btot[:, :], AF.Exp, bias=0.0, scale=-1.0 / 16.0),
        reads=("btot",), writes=("eblB",))

    def emit_tile_loads(t):
        P.d("sync", lambda e, t=t: e.dma_start(
            out=ob3[:, :, :],
            in_=scr_o[t, :, :, :].rearrange("b p f -> p b f")),
            reads=("obuf", "scr_o"), writes=("obuf",))
        P.d("sync", lambda e, t=t: e.dma_start(
            out=gb3[:, :, :],
            in_=scr_g[t, :, :, :].rearrange("b p f -> p b f")),
            reads=("gbuf", "scr_g"), writes=("gbuf",))
    def emit_qeb_load(t, hb):
        for h in range(H):
            P.d("sync", lambda e, t=t, h=h, hb=hb: e.dma_start(
                out=qeb3[:, 4 * h:4 * (h + 1), :],
                in_=scr_qe[h, t, :, :]
                    .rearrange("p (d f) -> p d f", f=TA)
                    [:, :, CH * hb:CH * (hb + 1)]),
                reads=("qebuf", "scr_qe"), writes=("qebuf",))

    emit_tile_loads(0)
    emit_qeb_load(0, 0)
    def mk_ln4(t):
        def th():
            for tb in range(4):
                for i in range(2):
                    P.c("vector", lambda e, tb=tb, i=i: e.bn_stats(
                        statS[5][:, 6 * i:6 * (i + 1)],
                        ffn043[:, tb, 512 * i:512 * (i + 1)]),
                        reads=("ynfm",), writes=("stat5",))
                    P.c("vector", lambda e, tb=tb, i=i: e.bn_stats(
                        statS[5][:, 12 + 6 * i:12 + 6 * (i + 1)],
                        ffn143[:, tb, 512 * i:512 * (i + 1)]),
                        reads=("hfm",), writes=("stat5",))
                P.c("vector", lambda e: e.bn_aggr(
                    mvS[5][:, :], statS[5][:, 0:24]),
                    reads=("stat5",), writes=("stat5",))
                P.c("scalar", lambda e: e.activation(
                    t1S[5][:, :], mvS[5][:, 1:2], AF.Ln, bias=EPS,
                    scale=1.0),
                    reads=("stat5",), writes=("rb5",))
                P.c("scalar", lambda e: e.activation(
                    rS[5][:, :], t1S[5][:, :], AF.Exp, bias=0.0,
                    scale=-0.5),
                    reads=("rb5",), writes=("rb5",))
                P.c("vector", lambda e: e.tensor_scalar(
                    bS[5][:, :], mvS[5][:, 0:1], rS[5][:, :], -1.0,
                    ALU.mult, ALU.mult),
                    reads=("stat5", "rb5"), writes=("rb5",))
                P.c("scalar", lambda e, tb=tb: e.activation(
                    ffn043[:, tb, :], ffn043[:, tb, :], AF.Identity,
                    bias=bS[5][:, :], scale=rS[5][:, :]),
                    reads=("ynfm", "rb5"), writes=("ynfm",))
                P.c("vector", lambda e, tb=tb: e.tensor_tensor(
                    ffn043[:, tb, :], ffn043[:, tb, :],
                    xyb43[:, tb, 0:1024], ALU.add),
                    reads=("ynfm", "hfm"), writes=("ynfm",))
                P.d("scalar", lambda e, t=t, tb=tb: e.dma_start(
                    out=out_hbm[TB * t + 128 * tb:TB * t + 128 * (tb + 1),
                                0:1024],
                    in_=ffn043[:, tb, :]),
                    reads=("ynfm",), writes=("out",))
                P.c("scalar", lambda e, tb=tb: e.activation(
                    ffn143[:, tb, :], ffn143[:, tb, :], AF.Identity,
                    bias=bS[5][:, :], scale=rS[5][:, :]),
                    reads=("hfm", "rb5"), writes=("hfm",))
                P.c("vector", lambda e, tb=tb: e.tensor_tensor(
                    ffn143[:, tb, :], ffn143[:, tb, :],
                    xyb43[:, tb, 1024:2048], ALU.add),
                    reads=("hfm",), writes=("hfm",))
                P.d("scalar", lambda e, t=t, tb=tb: e.dma_start(
                    out=out_hbm[TB * t + 128 * tb:TB * t + 128 * (tb + 1),
                                1024:2048],
                    in_=ffn143[:, tb, :]),
                    reads=("hfm",), writes=("out",))
        return th

    pend_ln4 = [None]

    for t in range(NTB):
        P.c("scalar", lambda e: e.activation(gbuf[:, :], gbuf[:, :], AF.Silu),
            reads=("gbuf",), writes=("gbuf",))
        P.c("vector", lambda e: e.tensor_tensor(
            gb3[:, :, :], gb3[:, :, :],
            wnormbc[:, :].unsqueeze(1).broadcast_to([128, 4, VD]),
            ALU.mult),
            reads=("gbuf", "wnormbc"), writes=("gbuf",))
        for hb in range(2):
            gchunk = 2 * t + hb
            for h in range(H):
                for dsl in range(4):
                    P.c("vector", lambda e, h=h, dsl=dsl: e.tensor_copy(
                        Spb3[:, dsl, :], Sp3[:, 4 * h + dsl, :]),
                        reads=(f"Sp{h}",), writes=("Spb",))
                for cb in range(2):
                    for dsl in range(4):
                        P.c("tensor", lambda e, h=h, hb=hb, cb=cb, dsl=dsl:
                            e.matmul(
                                ps[4][:, 0:DVH],
                                qeb3[:, 4 * h + dsl,
                                     128 * cb:128 * (cb + 1)],
                                Spb3[:, dsl, :],
                                start=(dsl == 0), stop=(dsl == 3)),
                            reads=("qebuf", "Spb"), writes=(psr[4],))
                    P.c("vector", lambda e, h=h, hb=hb, cb=cb: e.tensor_tensor(
                        ob3[:, 2 * hb + cb, DVH * h:DVH * (h + 1)],
                        ob3[:, 2 * hb + cb, DVH * h:DVH * (h + 1)],
                        ps[4][:, 0:DVH], ALU.add),
                        reads=("obuf", psr[4]), writes=("obuf",))
            if hb == 0:
                emit_qeb_load(t, 1)
            elif t + 1 < NTB:
                emit_qeb_load(t + 1, 0)
            rmsE, rmsR = rmsEs[hb], rmsRs[hb]
            rres = f"rms{hb}"
            for i in range(8):
                blk, h = i // 4, i % 4
                P.c("vector", lambda e, blk=blk, h=h, i=i, hb=hb: e.bn_stats(
                    statS[4][:, 6 * i:6 * (i + 1)],
                    ob3[:, 2 * hb + blk, DVH * h:DVH * (h + 1)]),
                    reads=("obuf",), writes=("stat4",))
            for i in range(8):
                P.c("vector", lambda e, i=i: e.bn_aggr(
                    mvS[4][:, :], statS[4][:, 6 * i:6 * (i + 1)]),
                    reads=("stat4",), writes=("stat4m",))
                P.c("vector", lambda e, i=i, rmsE=rmsE: e.scalar_tensor_tensor(
                    rmsE[:, i:i + 1], mvS[4][:, 0:1], mvS[4][:, 0:1],
                    mvS[4][:, 1:2], ALU.mult, ALU.add),
                    reads=("stat4m",), writes=(rres,))
            P.c("scalar", lambda e, rmsE=rmsE, rmsR=rmsR: e.activation(
                rmsR[:, :], rmsE[:, :], AF.Ln, bias=EPS, scale=1.0),
                reads=(rres,), writes=(rres,))
            P.c("scalar", lambda e, rmsR=rmsR: e.activation(
                rmsR[:, :], rmsR[:, :], AF.Exp, bias=0.0, scale=-0.5),
                reads=(rres,), writes=(rres,))
            for i in range(8):
                blk, h = i // 4, i % 4
                P.c("vector", lambda e, blk=blk, h=h, i=i, hb=hb, rmsR=rmsR:
                    e.scalar_tensor_tensor(
                        oftm3[:, blk, DVH * h:DVH * (h + 1)],
                        ob3[:, 2 * hb + blk, DVH * h:DVH * (h + 1)],
                        rmsR[:, i:i + 1],
                        gb3[:, 2 * hb + blk, DVH * h:DVH * (h + 1)],
                        ALU.mult, ALU.mult),
                    reads=("obuf", rres, "gbuf"), writes=("mixA",))
            for blk in range(2):
                coff = CH * hb + 128 * blk
                for half in range(2):
                    bank = 6 + half
                    for i in range(4):
                        uc = 4 * half + i
                        P.c("tensor", lambda e, blk=blk, uc=uc, i=i, bank=bank:
                            e.transpose(
                                psb[bank][:, 128 * i:128 * (i + 1)],
                                oftm3[:, blk, 128 * uc:128 * (uc + 1)],
                                idb[:, :]),
                            reads=("mixA", "idb"), writes=(psr[bank],))
                    P.c("vector", lambda e, half=half, coff=coff, bank=bank:
                        e.tensor_copy(
                            offm3[:, 4 * half:4 * (half + 1), coff:coff + 128],
                            psb[bank][:, 0:512]
                            .rearrange("p (b f) -> p b f", f=128)),
                        reads=(psr[bank],), writes=("offm",))
            for h in range(H):
                for dsl in range(4):
                    col = h * 64 + dsl * 16 + gchunk
                    P.c("vector", lambda e, h=h, dsl=dsl, col=col:
                        e.tensor_scalar(
                            Sp3[:, 4 * h + dsl, :], Sp3[:, 4 * h + dsl, :],
                            eblB[:, col:col + 1], None, ALU.mult),
                        reads=(f"Sp{h}", "eblB"), writes=(f"Sp{h}",))
        if pend_ln4[0] is not None:
            pend_ln4[0]()
            pend_ln4[0] = None
        if t + 1 < NTB:
            emit_tile_loads(t + 1)
        for pair in range(2):
            for uc in range(8):
                slot = uc % 2
                P.d("sync", lambda e, uc=uc, slot=slot: e.dma_start(
                    out=wob3[:, slot, :],
                    in_=scr_wo[128 * uc:128 * (uc + 1), :]),
                    reads=(f"wob{slot}", "scr_wo"), writes=(f"wob{slot}",))
                for tb2 in range(2):
                    tb = 2 * pair + tb2
                    for eg in range(4):
                        bank = 4 * tb2 + eg
                        P.c("tensor",
                            lambda e, uc=uc, slot=slot, tb=tb, eg=eg, bank=bank:
                            e.matmul(
                                ps[bank][:, :],
                                offm3[:, uc, 128 * tb:128 * (tb + 1)],
                                wob3[:, slot, 512 * eg:512 * (eg + 1)],
                                start=(uc == 0), stop=(uc == 7)),
                            reads=("offm", f"wob{slot}"), writes=(psr[bank],))
            for tb2 in range(2):
                tb = 2 * pair + tb2
                P.d("gpsimd", lambda e, t=t, tb=tb: e.dma_start(
                    out=xyb[:, :],
                    in_=x_in[TB * t + 128 * tb:TB * t + 128 * (tb + 1), :]),
                    reads=("xyb",), writes=("xyb",))
                rB, bB, rres = ln_stats(
                    lambda i, tb2=tb2: ps[4 * tb2 + i][:, :], 4,
                    (psr[4 * tb2], psr[4 * tb2 + 1], psr[4 * tb2 + 2],
                     psr[4 * tb2 + 3]), si=tb2)
                for eg in range(4):
                    bank = 4 * tb2 + eg
                    P.c("vector", lambda e, eg=eg, bank=bank, rB=rB:
                        e.scalar_tensor_tensor(
                            xyb[:, 512 * eg:512 * (eg + 1)],
                            ps[bank][:, :], rB[:, :],
                            xyb[:, 512 * eg:512 * (eg + 1)],
                            ALU.mult, ALU.add),
                        reads=(psr[bank], rres, "xyb"), writes=("xyb",))
                P.c("vector", lambda e, bB=bB: e.tensor_scalar(
                    xyb[:, :], xyb[:, :], bB[:, :], None, ALU.add),
                    reads=("xyb", rres), writes=("xyb",))
                P.d("scalar", lambda e, t=t, tb=tb: e.dma_start(
                    out=scr_y[t, tb, :, :], in_=xyb[:, :]),
                    reads=("xyb",), writes=("scr_y",))
                rB3, bB3, rres3 = ln_stats(
                    lambda i: xyb[:, 512 * i:512 * (i + 1)], 4, ("xyb",),
                    si=2 + tb2)
                for rnd in range(4):
                    P.c("scalar", lambda e, rnd=rnd, rB3=rB3, bB3=bB3:
                        e.activation(
                            yntm[:, 512 * rnd:512 * (rnd + 1)],
                            xyb[:, 512 * rnd:512 * (rnd + 1)], AF.Identity,
                            bias=bB3[:, :], scale=rB3[:, :]),
                        reads=("xyb", rres3), writes=("mixB",))
                    bank = 4 * tb2 + rnd % 2
                    for i in range(4):
                        P.c("tensor", lambda e, rnd=rnd, i=i, bank=bank:
                            e.transpose(
                                psb[bank][:, 128 * i:128 * (i + 1)],
                                yntm[:, 512 * rnd + 128 * i:
                                     512 * rnd + 128 * (i + 1)],
                                idb[:, :]),
                            reads=("mixB", "idb"), writes=(psr[bank],))
                    P.c("vector", lambda e, rnd=rnd, tb=tb, bank=bank:
                        e.tensor_copy(
                            ynfm3[:, 4 * rnd:4 * (rnd + 1),
                                  128 * tb:128 * (tb + 1)],
                            psb[bank][:, 0:512]
                            .rearrange("p (b f) -> p b f", f=128)),
                        reads=(psr[bank],), writes=("ynfm",))
        wup_all4 = wup_all.rearrange("(k m p) f -> k m p f", m=16, p=128)
        for mg in range(16):
            pb = 4 * (mg % 2)
            for jj in range(4):
                slot = jj % 2
                P.d("sync", lambda e, mg=mg, jj=jj, slot=slot: e.dma_start(
                    out=upb4[:, slot, :, :],
                    in_=wup_all4[4 * jj:4 * (jj + 1), mg, :, :]
                        .rearrange("k p f -> p k f")),
                    reads=(f"upb{slot}", "wup_all"), writes=(f"upb{slot}",))
                for kk in range(4):
                    k = 4 * jj + kk
                    for mb in range(4):
                        P.c("tensor",
                            lambda e, slot=slot, kk=kk, k=k, mb=mb, pb=pb:
                            e.matmul(
                                ps[pb + mb][:, :],
                                upb4[:, slot, kk, 128 * mb:128 * (mb + 1)],
                                ynfm3[:, k, :],
                                start=(k == 0), stop=(k == 15)),
                            reads=(f"upb{slot}", "ynfm"),
                            writes=(psr[pb + mb],))
            for mb in range(4):
                m = 4 * mg + mb
                bank = pb + mb
                P.c("scalar", lambda e, m=m, bank=bank: e.activation(
                    glu_s[:, :], ps[bank][:, :], AF.Square,
                    bias=ubf[:, m:m + 1], scale=1.0),
                    reads=(psr[bank], "ubf"), writes=("mixA",))
                P.c("vector", lambda e: e.tensor_scalar(
                    glu_u[:, :], glu_s[:, :], GC2, GC1, ALU.mult, ALU.add),
                    reads=("mixA",), writes=("mixA",))
                P.c("vector", lambda e, m=m, bank=bank: e.scalar_tensor_tensor(
                    glu_z[:, :], ps[bank][:, :], ubf[:, m:m + 1], glu_u[:, :],
                    ALU.add, ALU.mult),
                    reads=(psr[bank], "ubf", "mixA"), writes=("mixB",))
                P.c("scalar", lambda e: e.activation(
                    glu_s[:, :], glu_z[:, :], AF.Sigmoid),
                    reads=("mixB",), writes=("mixA",))
                P.c("vector", lambda e, m=m, bank=bank: e.scalar_tensor_tensor(
                    hfm3[:, m, :], ps[bank][:, :], ubf[:, m:m + 1],
                    glu_s[:, :], ALU.add, ALU.mult),
                    reads=(psr[bank], "ubf", "mixA"), writes=("hfm",))
        wdn_all4 = wdn_all.rearrange("(k h p) f -> k h p f", h=2, p=128)
        for eh in range(2):
            for jj in range(32):
                slot = jj % 2
                P.d("sync", lambda e, eh=eh, jj=jj, slot=slot: e.dma_start(
                    out=dnb4[:, slot, :, :],
                    in_=wdn_all4[2 * jj:2 * (jj + 1), eh, :, :]
                        .rearrange("k p f -> p k f")),
                    reads=(f"dnb{slot}", "wdn_all"), writes=(f"dnb{slot}",))
                for kk in range(2):
                    k = 2 * jj + kk
                    for tb in range(4):
                        for half in range(2):
                            bank = 2 * tb + half
                            P.c("tensor",
                                lambda e, slot=slot, kk=kk, k=k, tb=tb,
                                half=half, bank=bank: e.matmul(
                                    ps[bank][:, :],
                                    hfm3[:, k, 128 * tb:128 * (tb + 1)],
                                    dnb4[:, slot, kk,
                                         512 * half:512 * (half + 1)],
                                    start=(k == 0), stop=(k == 63)),
                                reads=(f"dnb{slot}", "hfm"),
                                writes=(psr[bank],))
            if eh == 0:
                for tb in range(4):
                    for half in range(2):
                        bank = 2 * tb + half
                        P.c("vector", lambda e, tb=tb, half=half, bank=bank:
                            e.tensor_tensor(
                                ffn043[:, tb, 512 * half:512 * (half + 1)],
                                ps[bank][:, :],
                                dbbc[:, 512 * half:512 * (half + 1)], ALU.add),
                            reads=(psr[bank], "dbbc"),
                            writes=("ynfm",))
            else:
                for tb in (2, 3, 0, 1):
                    for half in range(2):
                        bank = 2 * tb + half
                        P.c("vector", lambda e, tb=tb, half=half, bank=bank:
                            e.tensor_tensor(
                                ffn143[:, tb, 512 * half:512 * (half + 1)],
                                ps[bank][:, :],
                                dbbc[:, 1024 + 512 * half:
                                     1024 + 512 * (half + 1)], ALU.add),
                            reads=(psr[bank], "dbbc"),
                            writes=("hfm",))
                for tb in range(4):
                    P.d("gpsimd", lambda e, t=t, tb=tb: e.dma_start(
                        out=xyb43[:, tb, :], in_=scr_y[t, tb, :, :]),
                        reads=("scr_y",), writes=("hfm",))
                pend_ln4[0] = mk_ln4(t)

    if pend_ln4[0] is not None:
        pend_ln4[0]()
        pend_ln4[0] = None

    if DEBUG_TAPS:
        taps = [("t_xn", scr_xn, "scr_xn"), ("t_o", scr_o, "scr_o"),
                ("t_g", scr_g, "scr_g"), ("t_qe", scr_qe, "scr_qe"),
                ("t_y", scr_y, "scr_y"), ("t_st", st_loc, "st_loc"),
                ("t_wup", wup_loc, "wloc"), ("t_wdn", wdn_loc, "wloc")]
        for name, ap, res in taps:
            shp = list(ap.shape)
            tout = nc.dram_tensor(name, shp, ap.dtype, kind="ExternalOutput").ap()
            P.d("gpsimd", lambda e, tt=tout, a=ap: e.dma_start(out=tt, in_=a),
                reads=(res,), writes=())
        btout = nc.dram_tensor("t_bt", [128, H * 4 * NCH], F32,
                               kind="ExternalOutput").ap()
        P.d("gpsimd", lambda e: e.dma_start(out=btout, in_=btot[:, :]),
            reads=("btot",), writes=())

    P.finalize()
    fc = P.final_counts
    hw = {}
    with nc.Block() as block:

        @block.tensor
        def _(eng):
            P.replay("tensor", eng, sems, hw)

        @block.vector
        def _(eng):
            P.replay("vector", eng, sems, hw)

        @block.scalar
        def _(eng):
            P.replay("scalar", eng, sems, hw)

        @block.gpsimd
        def _(eng):
            P.replay("gpsimd", eng, sems, hw)

        @block.sync
        def _(eng):
            P.replay("sync", eng, sems, hw)
            for key, v in sorted(fc.items()):
                if key[0] in ("dma", "cc"):
                    eng.wait_ge(sems[key], v)

    nc._keepalive_stack = stack
    return nc


_CACHE = {}
_LAST_RES = None


def kernel(**inputs):
    x = np.asarray(inputs["L0_attn_x"], dtype=np.float32)
    wq = np.asarray(inputs["L0_attn_Wq"], dtype=np.float32)
    wk = np.asarray(inputs["L0_attn_Wk"], dtype=np.float32)
    wv = np.asarray(inputs["L0_attn_Wv"], dtype=np.float32)
    wgk1 = np.asarray(inputs["L0_attn_Wgk1"], dtype=np.float32)
    wgk2 = np.asarray(inputs["L0_attn_Wgk2"], dtype=np.float32)
    bgk2 = np.asarray(inputs["L0_attn_bgk2"], dtype=np.float32)
    wg = np.asarray(inputs["L0_attn_Wg"], dtype=np.float32)
    wnorm = np.asarray(inputs["L0_attn_Wnorm"], dtype=np.float32)
    wo = np.asarray(inputs["L0_attn_Wo"], dtype=np.float32)
    wup = np.asarray(inputs["L0_ffn_up_kernel"], dtype=np.float32)
    ub = np.asarray(inputs["L0_ffn_up_bias"], dtype=np.float32)
    wdn = np.asarray(inputs["L0_ffn_down_kernel"], dtype=np.float32)
    db = np.asarray(inputs["L0_ffn_down_bias"], dtype=np.float32)

    xf = np.ascontiguousarray(x.reshape(NCORE * TPC, D))
    bgk2f = np.ascontiguousarray(bgk2.reshape(16, 128).T)
    ubf = np.ascontiguousarray(ub.reshape(64, 128).T)
    dbbc = np.ascontiguousarray(np.broadcast_to(db[None, :], (128, D)))
    wnormbc = np.ascontiguousarray(np.tile(wnorm, (128, H)))
    idf = np.eye(128, dtype=np.float32)
    idb = np.eye(128, dtype=ml_dtypes.bfloat16)
    maskmv = (np.arange(128)[:, None] <= np.arange(CH)[None, :]).astype(np.float32)
    cvals = np.array([0.0, 1.0, EPS, LNS, -1.0, -0.5, 1.0 / 16.0, -1.0 / 16.0],
                     dtype=np.float32)
    cpoolv = np.ascontiguousarray(np.broadcast_to(cvals[None, :], (128, 8)))

    ck = ("nc", DEBUG_TAPS)
    if ck not in _CACHE:
        _CACHE[ck] = build_program()
    nc = _CACHE[ck]

    in_maps = []
    for c in range(NCORE):
        pm = np.zeros((128, NCORE), dtype=np.float32)
        if c % 2 == 1:
            pm[:, c - 1] = 1.0
        in_maps.append({
            "x": np.ascontiguousarray(xf[c * TPC:(c + 1) * TPC]),
            "wq": wq, "wk": wk, "wv": wv, "wgk1": wgk1, "wgk2": wgk2,
            "bgk2f": bgk2f, "wg": wg, "wnormbc": wnormbc, "wo": wo,
            "wupsl": np.ascontiguousarray(wup[c * 256:(c + 1) * 256]),
            "ubf": ubf,
            "wdnsl": np.ascontiguousarray(wdn[c * 1024:(c + 1) * 1024]),
            "dbbc": dbbc, "pmask": pm, "idf": idf, "idb": idb,
            "maskm": maskmv, "cpool": cpoolv,
        })
    res = run_bass_kernel_spmd(nc, in_maps, core_ids=list(range(NCORE)))
    global _LAST_RES
    _LAST_RES = res
    outs = [res.results[c]["out"] for c in range(NCORE)]
    return np.concatenate(outs, axis=0).reshape(4, 8192, D)


# version 22
# speedup vs baseline: 1.1143x; 1.1143x over previous
import math
import numpy as np
import ml_dtypes
import concourse.bass as bass
import concourse.mybir as mybir
from concourse.bass_utils import run_bass_kernel_spmd

F32 = mybir.dt.float32
BF16 = mybir.dt.bfloat16
I32 = mybir.dt.int32
AF = mybir.ActivationFunctionType
ALU = mybir.AluOpType
AXX = mybir.AxisListType.X

NCORE = 8
TPC = 4096
D = 2048
H = 4
DKH = 512
DVH = 256
VD = 1024
FF = 8192
TA = 512
NTA = TPC // TA
CH = 256
NCH = TPC // CH
TB = 512
NTB = TPC // TB
EPS = 1e-5
LNS = float(-0.5 * math.log(512.0))
GC1 = float(2.0 * math.sqrt(2.0 / math.pi))
GC2 = float(2.0 * math.sqrt(2.0 / math.pi) * 0.044715)
MAPBASE = 16512
SBUF_TOP = 229344
DEBUG_TAPS = False


DMA_SLOTS = 8


class Plan:
    def __init__(self):
        self.steps = []
        self.lastw = {}
        self.readers = {}
        self.dmaq = {}

    def emit(self, eng, fn, reads=(), writes=(), kind="c", tag=""):
        deps = set(self.fence_deps)
        for r in reads:
            w = self.lastw.get(r)
            if w is not None:
                deps.add(w)
        for w_ in writes:
            w = self.lastw.get(w_)
            if w is not None:
                deps.add(w)
            for rd in self.readers.get(w_, ()):
                deps.add(rd)
        i = len(self.steps)
        if kind == "d":
            q = self.dmaq.setdefault(eng, [])
            if len(q) >= DMA_SLOTS:
                deps.add(q[-DMA_SLOTS])
            q.append(i)
        self.steps.append(dict(i=i, eng=eng, fn=fn, deps=sorted(deps),
                               kind=kind, tag=tag, sem=None, val=None, inc=0))
        for r in reads:
            self.readers.setdefault(r, []).append(i)
        for w_ in writes:
            self.lastw[w_] = i
            self.readers[w_] = []
        return i

    def c(self, eng, fn, reads=(), writes=(), tag=""):
        return self.emit(eng, fn, reads, writes, "c", tag)

    def d(self, eng, fn, reads=(), writes=(), tag=""):
        return self.emit(eng, fn, reads, writes, "d", tag)

    def cc(self, fn, reads=(), writes=(), tag=""):
        return self.emit("gpsimd", fn, reads, writes, "cc", tag)

    def fence(self):
        deps = set()
        last = {}
        for s in self.steps:
            if s["kind"] != "d":
                last[(s["eng"], s["kind"])] = s["i"]
        deps.update(last.values())
        for q in self.dmaq.values():
            deps.update(q[-DMA_SLOTS:])
        self.fence_deps = sorted(deps)

    fence_deps = ()

    def finalize(self):
        dmaidx = {}
        for s in self.steps:
            if s["kind"] == "d":
                qi = dmaidx.get(s["eng"], 0)
                dmaidx[s["eng"]] = qi + 1
                s["klass"] = ("dma", s["eng"], qi % DMA_SLOTS)
            elif s["kind"] == "cc":
                s["klass"] = ("cc",)
            else:
                s["klass"] = ("c", s["eng"])
        need = set()
        for s in self.steps:
            grouped = {}
            for di in s["deps"]:
                ds = self.steps[di]
                if (ds["eng"] == "tensor" and s["eng"] == "tensor"
                        and ds["kind"] == "c" and s["kind"] == "c"):
                    continue
                k = ds["klass"]
                if grouped.get(k, -1) < di:
                    grouped[k] = di
            s["wdeps"] = sorted(grouped.values())
            need.update(s["wdeps"])
        ctr = {}
        for s in self.steps:
            if s["kind"] == "d":
                key = s["klass"]
                ctr[key] = ctr.get(key, 0) + 16
                s["sem"], s["val"], s["inc"] = key, ctr[key], 16
            elif s["kind"] == "cc":
                key = s["klass"]
                ctr[key] = ctr.get(key, 0) + 1
                s["sem"], s["val"], s["inc"] = key, ctr[key], 1
            elif s["i"] in need:
                key = s["klass"]
                ctr[key] = ctr.get(key, 0) + 1
                s["sem"], s["val"], s["inc"] = key, ctr[key], 1
        self.final_counts = dict(ctr)

    def replay(self, eng_name, eng, sems, hw):
        for s in self.steps:
            if s["eng"] != eng_name:
                continue
            wants = {}
            for di in s["wdeps"]:
                ds = self.steps[di]
                if ds["sem"] is None:
                    continue
                k, v = ds["sem"], ds["val"]
                if wants.get(k, 0) < v:
                    wants[k] = v
            for k in sorted(wants):
                v = wants[k]
                if hw.get((eng_name, k), 0) >= v:
                    continue
                eng.wait_ge(sems[k], v)
                hw[(eng_name, k)] = v
            inst = s["fn"](eng)
            if s["inc"]:
                inst.then_inc(sems[s["sem"]], s["inc"])


def build_program():
    import contextlib
    nc = bass.Bass("TRN2", target_bir_lowering=False, debug=False)
    stack = contextlib.ExitStack()

    def inp(name, shape, dtype=F32):
        return nc.dram_tensor(name, list(shape), dtype, kind="ExternalInput").ap()

    x_in = inp("x", [TPC, D])
    wq_in = inp("wq", [D, D])
    wk_in = inp("wk", [D, D])
    wv_in = inp("wv", [D, VD])
    wgk1_in = inp("wgk1", [D, 16])
    wgk2_in = inp("wgk2", [16, D])
    bgk2_in = inp("bgk2f", [128, 16])
    wg_in = inp("wg", [D, VD])
    wnormbc_in = inp("wnormbc", [128, VD])
    wo_in = inp("wo", [VD, D])
    wupsl_in = inp("wupsl", [256, FF])
    ubf_in = inp("ubf", [128, 64])
    wdnsl_in = inp("wdnsl", [1024, D])
    dbbc_in = inp("dbbc", [128, D])
    pmask_in = inp("pmask", [128, NCORE])
    idf_in = inp("idf", [128, 128])
    idb_in = inp("idb", [128, 128], BF16)
    maskm_in = inp("maskm", [128, CH])
    cpool_in = inp("cpool", [128, 8])

    out_hbm = nc.dram_tensor("out", [TPC, D], F32, kind="ExternalOutput").ap()

    scr_xn = nc.dram_tensor("scr_xn", [NTA, 128, 16 * TA], BF16).ap()
    scr_qe = nc.dram_tensor("scr_qe", [H, NTA, 128, 4 * TA], BF16).ap()
    scr_o = nc.dram_tensor("scr_o", [NTA, 4, 128, VD], F32).ap()
    scr_g = nc.dram_tensor("scr_g", [NTA, 4, 128, VD], F32).ap()
    scr_y = nc.dram_tensor("scr_y", [NTB, 4, 128, D], F32).ap()
    scr_wo = nc.dram_tensor("scr_wo", [VD, D], BF16).ap()
    scr_wq = nc.dram_tensor("scr_wq", [D, D], BF16).ap()
    scr_wk = nc.dram_tensor("scr_wk", [D, D], BF16).ap()
    scr_wv = nc.dram_tensor("scr_wv", [D, VD], BF16).ap()
    scr_wg = nc.dram_tensor("scr_wg", [D, VD], BF16).ap()
    wup_loc = nc.dram_tensor("wup_loc", [2 * 16 * 128, 512], BF16).ap()
    wup_all = nc.dram_tensor("wup_all", [16 * 16 * 128, 512], BF16).ap()
    wdn_loc = nc.dram_tensor("wdn_loc", [8 * 2 * 128, 1024], BF16).ap()
    wdn_all = nc.dram_tensor("wdn_all", [64 * 2 * 128, 1024], BF16).ap()
    st_loc = nc.dram_tensor("st_loc", [H * 4 * 128, DVH], F32).ap()
    st_all_h = [nc.dram_tensor(f"st_all{h}", [NCORE * 4 * 128, DVH], F32).ap()
                for h in range(H)]

    _off = [MAPBASE]

    def sb(name, shape, dtype, at=None):
        esz = {F32: 4, BF16: 2, I32: 4}[dtype]
        nbytes = int(np.prod(shape[1:])) * esz
        if at is None:
            at = _off[0]
            _off[0] = at + ((nbytes + 31) // 32) * 32
        assert at + nbytes <= SBUF_TOP, (name, at, nbytes)
        return nc.alloc_sbuf_tensor_at(name, list(shape), dtype, offset=at).ap()

    cpool = sb("cpool", [128, 8], F32)
    idf = sb("idf", [128, 128], F32)
    idb = sb("idb", [128, 128], BF16)
    maskm = sb("maskm", [128, CH], F32)
    wnormbc = sb("wnormbc", [128, VD], F32)
    bgk2f = sb("bgk2f", [128, 16], F32)
    nbgk2f = sb("nbgk2f", [128, 16], F32)
    pmask = sb("pmask", [128, NCORE], F32)
    btot = sb("btot", [128, H * 4 * NCH], F32)
    eblB = sb("eblB", [128, H * 4 * NCH], F32)
    eblA = sb("eblA", [128, 8], F32)
    eblA2 = sb("eblA2", [128, 8], F32)
    ubf = sb("ubf", [128, 64], F32)
    dbbc = sb("dbbc", [128, D], F32)
    zeros = sb("zeros", [128, CH], F32)
    NSET = 6
    statS = [sb(f"statb{i}", [128, 48], F32) for i in range(NSET)]
    mvS = [sb(f"mvb{i}", [128, 2], F32) for i in range(NSET)]
    rS = [sb(f"rcol{i}", [128, 1], F32) for i in range(NSET)]
    bS = [sb(f"bcol{i}", [128, 1], F32) for i in range(NSET)]
    t1S = [sb(f"lnt1{i}", [128, 1], F32) for i in range(NSET)]
    rmsEs = [sb(f"rmsE{i}", [128, 8], F32) for i in range(2)]
    rmsRs = [sb(f"rmsR{i}", [128, 8], F32) for i in range(2)]
    ARENA = _off[0]

    _off[0] = ARENA
    wqs = sb("wqs", [128, 16 * DKH], BF16)
    wks = sb("wks", [128, 16 * DKH], BF16)
    wvs = sb("wvs", [128, 16 * DVH], BF16)
    wgs = sb("wgs", [128, 16 * DVH], BF16)
    wgk2s = sb("wgk2s", [16, DKH], BF16)
    wgk1s = sb("wgk1s", [128, 16 * 16], BF16)
    z1f = sb("z1f", [16, TPC], BF16)
    xnf_b0 = sb("xnf_b0", [128, 16 * TA], BF16)
    xnf_b1 = sb("xnf_b1", [128, 16 * TA], BF16)
    gkf = sb("gkf", [128, 4 * TA], F32)
    bpr = sb("bpr", [128, 4 * TA], F32)
    enb = sb("enb", [128, 4 * TA], F32)
    qe_b0 = sb("qe_b0", [128, 4 * TA], BF16)
    qe_b1 = sb("qe_b1", [128, 4 * TA], BF16)
    kd_b0 = sb("kd_b0", [128, 4 * TA], BF16)
    kd_b1 = sb("kd_b1", [128, 4 * TA], BF16)
    kdt = sb("kdt", [128, 2 * DKH], BF16)
    vt_b0 = sb("vt_b0", [128, 4 * DVH], BF16)
    vt_b1 = sb("vt_b1", [128, 4 * DVH], BF16)
    gt = sb("gt", [128, 4 * DVH], F32)
    Ab = sb("Ab", [128, 2 * CH], BF16)
    S0 = sb("S0", [128, 4 * DVH], F32)
    S1 = sb("S1", [128, 4 * DVH], F32)
    Sbuf = sb("Sbuf", [128, 4 * DVH], BF16)
    ot = sb("ot", [128, 4 * DVH], F32)
    XLD_OFF = _off[0]
    xld = sb("xld", [128, 2 * D], F32)
    xnt = sb("xnt", [128, D], BF16)
    A_END = _off[0]

    _off[0] = ARENA
    hfm = sb("hfm", [128, 64 * TB], BF16)
    YNFM_OFF = _off[0]
    ynfm = sb("ynfm", [128, 16 * TB], BF16)
    obuf = sb("obuf", [128, 4 * VD], F32)
    gbuf = sb("gbuf", [128, 4 * VD], F32)
    qebuf = sb("qebuf", [128, 16 * CH], BF16)
    Sp = sb("Sp", [128, 16 * DVH], F32)
    Spb = sb("Spb", [128, 4 * DVH], BF16)
    MIXA_OFF = _off[0]
    oftm = sb("oftm", [128, 2 * VD], BF16)
    MIXB_OFF = _off[0]
    yntm = sb("yntm", [128, D], BF16)
    offm = sb("offm", [128, 8 * TB], BF16)
    xyb = sb("xyb", [128, D], F32)
    WOB_OFF = _off[0]
    wob = sb("wob", [128, 2 * D], BF16)
    upb = sb("upb", [128, 2 * 4 * 512], BF16)
    dnb = sb("dnb", [128, 2 * 2 * 1024], BF16)
    B_END = _off[0]
    assert max(A_END, B_END) <= SBUF_TOP - 64, (A_END, B_END)
    glu_s = nc.alloc_sbuf_tensor_at("glu_s", [128, 512], F32,
                                    offset=MIXA_OFF).ap()
    glu_u = nc.alloc_sbuf_tensor_at("glu_u", [128, 512], F32,
                                    offset=MIXA_OFF + 2048).ap()
    glu_z = nc.alloc_sbuf_tensor_at("glu_z", [128, 512], F32,
                                    offset=MIXB_OFF).ap()
    ffn0 = nc.alloc_sbuf_tensor_at("ffn0", [128, 4 * 1024], F32,
                                   offset=MIXA_OFF).ap()
    ffn1 = nc.alloc_sbuf_tensor_at("ffn1", [128, 1024], F32,
                                   offset=WOB_OFF).ap()
    ffn04 = nc.alloc_sbuf_tensor_at("ffn04", [128, 4 * VD], F32,
                                    offset=YNFM_OFF).ap()
    ffn14 = nc.alloc_sbuf_tensor_at("ffn14", [128, 4 * VD], F32,
                                    offset=ARENA).ap()
    xyb4 = nc.alloc_sbuf_tensor_at("xyb4", [128, 4 * D], F32,
                                   offset=ARENA + 16384).ap()

    st_tmp = nc.alloc_sbuf_tensor_at("st_tmp", [128, 4 * DVH], F32,
                                     offset=ARENA).ap()
    st_tmp2 = nc.alloc_sbuf_tensor_at("st_tmp2", [128, 4 * DVH], F32,
                                      offset=YNFM_OFF).ap()
    _off[0] = A_END
    dq_row = nc.alloc_sbuf_tensor_at("dq_row", [128, 4096], F32,
                                     offset=XLD_OFF).ap()
    dq_wt = sb("dq_wt", [128, 512], F32)
    dq_am = sb("dq_am", [128, 16], F32)
    dq_bi = sb("dq_bi", [128, 16], I32)
    dq_sc = sb("dq_sc", [128, 16], F32)
    dq_iv = sb("dq_iv", [128, 16], F32)
    dq_x = sb("dq_x", [128, 512], F32)
    dq_a = sb("dq_a", [128, 512], F32)
    dq_f = sb("dq_f", [128, 512], F32)
    dq_c = sb("dq_c", [128, 512], F32)
    dq_q = sb("dq_q", [128, 512], BF16)
    dq_st = sb("dq_st", [128, 512], BF16)
    A_END2 = _off[0]
    assert A_END2 <= SBUF_TOP - 64, (A_END, B_END, A_END2)
    wo_cst = nc.alloc_sbuf_tensor_at("wo_cst", [128, D], BF16,
                                     offset=XLD_OFF).ap()

    ps = [stack.enter_context(nc.psum_tensor(f"psum{i}", [128, 512], F32))
          for i in range(8)]
    psb = [p.ap().bitcast(BF16) for p in ps]
    psr = [f"ps{i}" for i in range(8)]
    ps = [p.ap() for p in ps]

    sems = {}
    semlist = [(("c", "tensor"), "s_pe"), (("c", "vector"), "s_dv"),
               (("c", "scalar"), "s_ac"), (("c", "gpsimd"), "s_gp"),
               (("cc",), "s_cc")]
    for q in ("sync", "scalar", "gpsimd"):
        for j in range(DMA_SLOTS):
            semlist.append((("dma", q, j), f"d_{q[:2]}{j}"))
    for key, name in semlist:
        sems[key] = stack.enter_context(nc.semaphore(name))

    CPOOL_VALS = [0.0, 1.0, EPS, LNS, -1.0, -0.5, 1.0 / 16.0, -1.0 / 16.0]
    for i, v in enumerate(CPOOL_VALS):
        nc.const_aps.register(F32, v, cpool[:, i:i + 1])

    P = Plan()

    dbg = {}
    if DEBUG_TAPS:
        for nm, shp, dty in [("d_sil", [128, 2 * VD], F32),
                             ("d_oftm", [2, 128, 2 * VD], F32),
                             ("d_offm", [128, 8 * TB], F32),
                             ("d_anb", [128, D], F32),
                             ("d_rmsR", [2, 128, 8], F32),
                             ("d_Sp", [128, 16 * DVH], F32),
                             ("d_ynfm", [128, 16 * TB], F32),
                             ("d_hfm", [128, 64 * TB], F32)]:
            dbg[nm] = nc.dram_tensor(nm, shp, dty, kind="ExternalOutput").ap()

    def tap(nm, ap, res, idx=None):
        if not DEBUG_TAPS:
            return
        dst = dbg[nm] if idx is None else dbg[nm][idx]
        P.d("gpsimd", lambda e, dst=dst, ap=ap: e.dma_start(out=dst, in_=ap),
            reads=(res,), writes=())

    def v3(ap, n, m):
        return ap.rearrange("p (a b) -> p a b", b=m)

    xnfs = [xnf_b0, xnf_b1]
    xnf3s = [v3(xnf_b0, 16, TA), v3(xnf_b1, 16, TA)]
    wqs3 = v3(wqs, 16, DKH)
    wks3 = v3(wks, 16, DKH)
    wvs3 = v3(wvs, 16, DVH)
    wgs3 = v3(wgs, 16, DVH)
    wgk1s3 = v3(wgk1s, 16, 16)
    gkf3 = v3(gkf, 4, TA)
    bpr3 = v3(bpr, 4, TA)
    enb3 = v3(enb, 4, TA)
    qes = [qe_b0, qe_b1]
    qe3s = [v3(qe_b0, 4, TA), v3(qe_b1, 4, TA)]
    kd3s = [v3(kd_b0, 4, TA), v3(kd_b1, 4, TA)]
    kdt3 = v3(kdt, 2, DKH)
    vt3s = [v3(vt_b0, 4, DVH), v3(vt_b1, 4, DVH)]
    eblAs = [eblA, eblA2]
    gt3 = v3(gt, 4, DVH)
    Ab3 = v3(Ab, 2, CH)
    S03 = v3(S0, 4, DVH)
    S13 = v3(S1, 4, DVH)
    Sb3 = v3(Sbuf, 4, DVH)
    ot3 = v3(ot, 4, DVH)
    xld3 = v3(xld, 2, D)
    hfm3 = v3(hfm, 64, TB)
    ynfm3 = v3(ynfm, 16, TB)
    ob3 = v3(obuf, 4, VD)
    gb3 = v3(gbuf, 4, VD)
    qeb3 = v3(qebuf, 16, CH)
    Sp3 = v3(Sp, 16, DVH)
    Spb3 = v3(Spb, 4, DVH)
    oftm3 = v3(oftm, 2, VD)
    offm3 = v3(offm, 8, TB)
    wob3 = v3(wob, 2, D)
    upb4 = upb.rearrange("p (s k f) -> p s k f", s=2, f=512)
    dnb4 = dnb.rearrange("p (s k f) -> p s k f", s=2, f=1024)
    ffn03 = v3(ffn0, 4, 1024)
    ffn043 = v3(ffn04, 4, VD)
    ffn143 = v3(ffn14, 4, VD)
    xyb43 = v3(xyb4, 4, D)
    st_tmp3 = v3(st_tmp, 4, DVH)
    st_tmp23 = v3(st_tmp2, 4, DVH)

    def ln_stats(src_fn, nchunks, rtag, si=0):
        sres, rres = f"stat{si}", f"rb{si}"
        sB, mB, rB, bB, tB = statS[si], mvS[si], rS[si], bS[si], t1S[si]
        for i in range(nchunks):
            P.c("vector", lambda e, i=i, sB=sB: e.bn_stats(
                sB[:, 6 * i:6 * (i + 1)], src_fn(i)),
                reads=rtag, writes=(sres,))
        P.c("vector", lambda e, n=nchunks, sB=sB, mB=mB: e.bn_aggr(
            mB[:, :], sB[:, 0:6 * n]),
            reads=(sres,), writes=(sres,))
        P.c("scalar", lambda e, mB=mB, tB=tB: e.activation(
            tB[:, :], mB[:, 1:2], AF.Ln, bias=EPS, scale=1.0),
            reads=(sres,), writes=(rres,))
        P.c("scalar", lambda e, tB=tB, rB=rB: e.activation(
            rB[:, :], tB[:, :], AF.Exp, bias=0.0, scale=-0.5),
            reads=(rres,), writes=(rres,))
        P.c("vector", lambda e, mB=mB, rB=rB, bB=bB: e.tensor_scalar(
            bB[:, :], mB[:, 0:1], rB[:, :], -1.0, ALU.mult, ALU.mult),
            reads=(sres, rres), writes=(rres,))
        return rB, bB, rres

    XLD_RES0 = ("xld0", "xld1")
    P.d("sync", lambda e: e.dma_start(out=cpool[:, :], in_=cpool_in),
        writes=("cpool",))
    P.fence()
    for dst, src, res in [(idf, idf_in, "idf"), (idb, idb_in, "idb"),
                          (maskm, maskm_in, "maskm"),
                          (wnormbc, wnormbc_in, "wnormbc"),
                          (bgk2f, bgk2_in, "bgk2f"), (pmask, pmask_in, "pmask"),
                          (ubf, ubf_in, "ubf"), (dbbc, dbbc_in, "dbbc")]:
        P.d("sync", lambda e, d=dst, s=src: e.dma_start(out=d[:, :], in_=s),
            writes=(res,))
    P.c("vector", lambda e: e.memset(zeros[:, :], 0.0), writes=("zeros",))
    P.c("vector", lambda e: e.tensor_scalar(
        nbgk2f[:, :], bgk2f[:, :], -1.0, None, ALU.mult),
        reads=("bgk2f",), writes=("nbgk2f",))
    def precast(w_in_, scr, nrow, ncol, res):
        for rr in range(nrow // 128):
            P.d("gpsimd", lambda e, rr=rr, w_in_=w_in_, ncol=ncol: e.dma_start(
                out=wo_cst[:, 0:ncol],
                in_=w_in_[128 * rr:128 * (rr + 1), :]),
                reads=XLD_RES0, writes=XLD_RES0)
            P.d("sync", lambda e, rr=rr, scr=scr, ncol=ncol: e.dma_start(
                out=scr[128 * rr:128 * (rr + 1), :], in_=wo_cst[:, 0:ncol]),
                reads=XLD_RES0, writes=(res,))

    precast(wo_in, scr_wo, VD, D, "scr_wo")
    precast(wq_in, scr_wq, D, D, "scr_wq")
    precast(wk_in, scr_wk, D, D, "scr_wk")
    precast(wv_in, scr_wv, D, VD, "scr_wv")
    precast(wg_in, scr_wg, D, VD, "scr_wg")

    XLD_RES = ("xld0", "xld1")

    def dq_load(w_sl, rc, c0, c1, d0):
        def th():
            P.d("sync", lambda e: e.dma_start(
                out=dq_row[:, d0:d0 + (c1 - c0)],
                in_=w_sl[128 * rc:128 * (rc + 1), c0:c1]),
                reads=XLD_RES, writes=XLD_RES)
        return th

    def dq_group_a(goff):
        def th():
            for i in range(4):
                P.c("tensor", lambda e, i=i: e.transpose(
                    ps[2][:, 128 * i:128 * (i + 1)],
                    dq_row[:, 512 * goff + 128 * i:512 * goff + 128 * (i + 1)],
                    idf[:, :]),
                    reads=XLD_RES + ("idf",), writes=(psr[2],))
            P.c("vector", lambda e: e.tensor_copy(dq_wt[:, :], ps[2][:, :]),
                reads=(psr[2],), writes=("dq_wt",))
            P.c("vector", lambda e: e.tensor_reduce(
                dq_am[:, :], dq_wt.rearrange("p (b k) -> p b k", k=32),
                axis=AXX, op=ALU.max, apply_absolute_value=True),
                reads=("dq_wt",), writes=("dq_am",))
            P.c("vector", lambda e: e.tensor_scalar(
                dq_am[:, :], dq_am[:, :], 1e-30, None, ALU.max),
                reads=("dq_am",), writes=("dq_am",))
            P.c("vector", lambda e: e.tensor_scalar(
                dq_bi[:, :], dq_am[:, :].bitcast(I32), 0x7F800000, None,
                ALU.bitwise_and),
                reads=("dq_am",), writes=("dq_bi",))
            P.c("vector", lambda e: e.tensor_scalar(
                dq_sc[:, :], dq_bi[:, :].bitcast(F32), 0.25, None, ALU.mult),
                reads=("dq_bi",), writes=("dq_sc",))
            P.c("vector", lambda e: e.tensor_scalar(
                dq_iv[:, :].bitcast(I32), dq_bi[:, :], -1, -0x80000000,
                ALU.mult, ALU.subtract),
                reads=("dq_bi",), writes=("dq_iv",))
            P.c("vector", lambda e: e.tensor_tensor(
                dq_x.rearrange("p (b k) -> p b k", k=32),
                dq_wt.rearrange("p (b k) -> p b k", k=32),
                dq_iv[:, :].unsqueeze(2).broadcast_to([128, 16, 32]),
                ALU.mult),
                reads=("dq_wt", "dq_iv"), writes=("dq_x",))
            P.c("scalar", lambda e: e.activation(dq_a[:, :], dq_x[:, :],
                                                 AF.Abs),
                reads=("dq_x",), writes=("dq_a",))
            P.c("vector", lambda e: e.tensor_scalar(
                dq_f[:, :], dq_a[:, :], 0.25, None, ALU.is_ge),
                reads=("dq_a",), writes=("dq_f",))
            for thr in (0.75, 1.25, 1.75):
                P.c("vector", lambda e, thr=thr: e.scalar_tensor_tensor(
                    dq_f[:, :], dq_a[:, :], thr, dq_f[:, :],
                    ALU.is_ge, ALU.add),
                    reads=("dq_a", "dq_f"), writes=("dq_f",))
            P.c("vector", lambda e: e.tensor_scalar(
                dq_c[:, :], dq_a[:, :], 2.5, None, ALU.is_ge),
                reads=("dq_a",), writes=("dq_c",))
            P.c("vector", lambda e: e.scalar_tensor_tensor(
                dq_f[:, :], dq_f[:, :], 0.5, dq_c[:, :], ALU.mult, ALU.add),
                reads=("dq_f", "dq_c"), writes=("dq_f",))
            P.c("vector", lambda e: e.scalar_tensor_tensor(
                dq_f[:, :], dq_a[:, :], 3.5, dq_f[:, :], ALU.is_ge, ALU.add),
                reads=("dq_a", "dq_f"), writes=("dq_f",))
            P.c("vector", lambda e: e.tensor_scalar(
                dq_c[:, :], dq_a[:, :], 5.0, None, ALU.is_ge),
                reads=("dq_a",), writes=("dq_c",))
            P.c("vector", lambda e: e.scalar_tensor_tensor(
                dq_f[:, :], dq_c[:, :], 2.0, dq_f[:, :], ALU.mult, ALU.add),
                reads=("dq_f", "dq_c"), writes=("dq_f",))
            P.c("scalar", lambda e: e.sign(dq_c[:, :], dq_x[:, :]),
                reads=("dq_x",), writes=("dq_c",))
            P.c("vector", lambda e: e.tensor_tensor(
                dq_f[:, :], dq_f[:, :], dq_c[:, :], ALU.mult),
                reads=("dq_f", "dq_c"), writes=("dq_f",))
            P.c("vector", lambda e: e.tensor_tensor(
                dq_q.rearrange("p (b k) -> p b k", k=32),
                dq_f.rearrange("p (b k) -> p b k", k=32),
                dq_sc[:, :].unsqueeze(2).broadcast_to([128, 16, 32]),
                ALU.mult),
                reads=("dq_f", "dq_sc"), writes=("dq_q",))
        return th

    def dq_group_b(dst_ap):
        def th():
            for i in range(4):
                P.c("tensor", lambda e, i=i: e.transpose(
                    psb[3][:, 128 * i:128 * (i + 1)],
                    dq_q[:, 128 * i:128 * (i + 1)], idb[:, :]),
                    reads=("dq_q", "idb"), writes=(psr[3],))
            P.c("vector", lambda e: e.tensor_copy(
                dq_st[:, :], psb[3][:, 0:512]),
                reads=(psr[3],), writes=("dq_st",))
            P.d("scalar", lambda e: e.dma_start(out=dst_ap, in_=dq_st[:, :]),
                reads=("dq_st",), writes=("wloc",))
        return th

    wup_loc3 = wup_loc.rearrange("(k p) f -> k p f", p=128)
    wdn_loc3 = wdn_loc.rearrange("(k p) f -> k p f", p=128)

    def _ag_wup():
        P.cc(lambda e: e.collective_compute(
            "AllGather", ALU.bypass, replica_groups=[list(range(NCORE))],
            ins=[wup_loc[:, :].opt()], outs=[wup_all[:, :].opt()]),
            reads=("wloc",), writes=("wup_all",), tag="ag_wup")

    def _ag_wdn():
        P.cc(lambda e: e.collective_compute(
            "AllGather", ALU.bypass, replica_groups=[list(range(NCORE))],
            ins=[wdn_loc[:, :].opt()], outs=[wdn_all[:, :].opt()]),
            reads=("wloc",), writes=("wdn_all",), tag="ag_wdn")

    dq_units = []
    _pend = []

    def _push(*ths):
        dq_units.append(_pend[:] + list(ths))
        _pend.clear()

    for rc in range(2):
        for half in range(2):
            base = 4096 * half
            _push(dq_load(wupsl_in, rc, base, base + 2048, 0))
            for g in range(8):
                if g == 2:
                    _pend.append(dq_load(wupsl_in, rc, base + 2048,
                                         base + 4096, 2048))
                _push(dq_group_a(g))
                _pend.append(dq_group_b(
                    wup_loc3[rc * 16 + 8 * half + g, :, :]))
    _push(_ag_wup)
    for rc in range(8):
        _push(dq_load(wdnsl_in, rc, 0, 2048, 0))
        for g in range(4):
            _push(dq_group_a(g))
            _pend.append(dq_group_b(
                wdn_loc3[rc * 2 + g // 2, :, 512 * (g % 2):512 * (g % 2 + 1)]))
    _push(_ag_wdn)

    dq_pos = [0]

    def dq_step():
        if dq_pos[0] < len(dq_units):
            for th in dq_units[dq_pos[0]]:
                th()
            dq_pos[0] += 1

    P.fence()

    P.d("gpsimd", lambda e: e.dma_start(
        out=v3(wgk1s, 16, 16), in_=wgk1_in.rearrange("(c p) f -> p c f", p=128)),
        writes=("wgk1s",))

    for t in range(NTA):
        par = t % 2
        xr = f"xnf{par}"
        xnf3 = xnf3s[par]
        for blk in range(4):
            sl = blk % 2
            P.d("sync", lambda e, t=t, blk=blk, sl=sl: e.dma_start(
                out=xld3[:, sl, :],
                in_=x_in[t * TA + 128 * blk:t * TA + 128 * (blk + 1), :]),
                reads=(f"xld{sl}",), writes=(f"xld{sl}",))
            rB, bB, rres = ln_stats(
                lambda i, sl=sl: xld3[:, sl, 512 * i:512 * (i + 1)],
                4, (f"xld{sl}",), si=blk % 2)
            P.c("scalar", lambda e, sl=sl, rB=rB, bB=bB: e.activation(
                xnt[:, :], xld3[:, sl, :], AF.Identity,
                bias=bB[:, :], scale=rB[:, :]),
                reads=(f"xld{sl}", rres), writes=("xnt",))
            for rnd in range(4):
                bank = 6 + rnd % 2
                for i in range(4):
                    P.c("tensor", lambda e, rnd=rnd, i=i, bank=bank: e.transpose(
                        psb[bank][:, 128 * i:128 * (i + 1)],
                        xnt[:, 512 * rnd + 128 * i:512 * rnd + 128 * (i + 1)],
                        idb[:, :]),
                        reads=("xnt", "idb"), writes=(psr[bank],))
                P.c("vector", lambda e, rnd=rnd, blk=blk, bank=bank,
                    xnf3=xnf3: e.tensor_copy(
                    xnf3[:, 4 * rnd:4 * (rnd + 1), 128 * blk:128 * (blk + 1)],
                    psb[bank][:, 0:512].rearrange("p (b f) -> p b f", f=128)),
                    reads=(psr[bank],), writes=(xr,))
        P.d("scalar", lambda e, t=t, par=par: e.dma_start(
            out=scr_xn[t, :, :], in_=xnfs[par][:, :]),
            reads=(xr,), writes=("scr_xn",))
        for kc in range(16):
            P.c("tensor", lambda e, kc=kc, xnf3=xnf3: e.matmul(
                ps[5][0:16, :], wgk1s3[:, kc, :], xnf3[:, kc, :],
                start=(kc == 0), stop=(kc == 15)),
                reads=("wgk1s", xr), writes=(psr[5],))
        P.c("vector", lambda e, t=t: e.tensor_copy(
            z1f[0:16, TA * t:TA * (t + 1)], ps[5][0:16, :]),
            reads=(psr[5],), writes=("z1f",))

    def load_head_weights(h):
        P.d("gpsimd", lambda e, h=h: e.dma_start(
            out=wgk2s[:, :], in_=wgk2_in[:, DKH * h:DKH * (h + 1)]),
            reads=("wgk2s",), writes=("wgk2s",))
        P.d("sync", lambda e, h=h: e.dma_start(
            out=wqs3[:, :, :],
            in_=scr_wq[:, DKH * h:DKH * (h + 1)]
                .rearrange("(c p) f -> p c f", p=128)),
            reads=("wqs", "scr_wq"), writes=("wqs",))
        P.d("sync", lambda e, h=h: e.dma_start(
            out=wks3[:, :, :],
            in_=scr_wk[:, DKH * h:DKH * (h + 1)]
                .rearrange("(c p) f -> p c f", p=128)),
            reads=("wks", "scr_wk"), writes=("wks",))
        P.d("sync", lambda e, h=h: e.dma_start(
            out=wvs3[:, :, :],
            in_=scr_wv[:, DVH * h:DVH * (h + 1)]
                .rearrange("(c p) f -> p c f", p=128)),
            reads=("wvs", "scr_wv"), writes=("wvs",))
        P.d("sync", lambda e, h=h: e.dma_start(
            out=wgs3[:, :, :],
            in_=scr_wg[:, DVH * h:DVH * (h + 1)]
                .rearrange("(c p) f -> p c f", p=128)),
            reads=("wgs", "scr_wg"), writes=("wgs",))

    def emit_proj(h, t):
        par = (h * NTA + t) % 2
        xr = f"xnf{par}"
        xnf3 = xnf3s[par]
        qr, kr_, vr, er = f"qe{par}", f"kd{par}", f"vt{par}", f"ebA{par}"
        qe3, kd3, vt3 = qe3s[par], kd3s[par], vt3s[par]
        eblAv = eblAs[par]
        for dsl in range(4):
            bank = dsl
            P.c("tensor", lambda e, h=h, t=t, dsl=dsl, bank=bank: e.matmul(
                ps[bank][:, :], wgk2s[0:16, 128 * dsl:128 * (dsl + 1)],
                z1f[0:16, TA * t:TA * (t + 1)], start=True, stop=True),
                reads=("wgk2s", "z1f"), writes=(psr[bank],))
            hd = h * 4 + dsl
            P.c("scalar", lambda e, dsl=dsl, hd=hd, bank=bank: e.activation(
                gkf3[:, dsl, :], ps[bank][:, :], AF.Abs,
                bias=bgk2f[:, hd:hd + 1], scale=1.0),
                reads=(psr[bank], "bgk2f"), writes=("gkf",))
            P.c("scalar", lambda e, dsl=dsl, hd=hd, bank=bank: e.activation(
                bpr3[:, dsl, :], ps[bank][:, :], AF.Relu,
                bias=nbgk2f[:, hd:hd + 1], scale=-1.0),
                reads=(psr[bank], "nbgk2f"), writes=("bpr",))
            P.c("scalar", lambda e, dsl=dsl: e.activation(
                enb3[:, dsl, :], gkf3[:, dsl, :], AF.Exp, bias=0.0,
                scale=-1.0),
                reads=("gkf",), writes=("enb",))
            P.c("scalar", lambda e, dsl=dsl: e.activation(
                gkf3[:, dsl, :], enb3[:, dsl, :], AF.Ln, bias=1.0,
                scale=1.0),
                reads=("enb",), writes=("gkf",))
            P.c("vector", lambda e, dsl=dsl: e.tensor_tensor(
                gkf3[:, dsl, :], gkf3[:, dsl, :], bpr3[:, dsl, :], ALU.add),
                reads=("gkf", "bpr"), writes=("gkf",))
        for dsl in range(4):
            for c in range(2):
                P.c("vector", lambda e, dsl=dsl, c=c: e.tensor_tensor_scan(
                    bpr3[:, dsl, CH * c:CH * (c + 1)],
                    gkf3[:, dsl, CH * c:CH * (c + 1)],
                    zeros[:, :], 0.0, ALU.add, ALU.bypass),
                    reads=("gkf", "zeros"), writes=("bpr",))
            col = h * 64 + dsl * 16 + 2 * t
            P.c("vector", lambda e, dsl=dsl, col=col: e.tensor_copy(
                btot[:, col:col + 2],
                bpr3[:, dsl, CH - 1::CH]),
                reads=("bpr",), writes=("btot",))
            P.c("scalar", lambda e, dsl=dsl, col=col, eblAv=eblAv:
                e.activation(
                    eblAv[:, 2 * dsl:2 * dsl + 2], btot[:, col:col + 2],
                    AF.Exp, bias=0.0, scale=-1.0 / 16.0),
                reads=("btot",), writes=(er,))
        for dsl in range(4):
            P.c("scalar", lambda e, dsl=dsl: e.activation(
                enb3[:, dsl, :], bpr3[:, dsl, :], AF.Exp, bias=0.0,
                scale=1.0 / 16.0),
                reads=("bpr",), writes=("enb",))
            P.c("scalar", lambda e, dsl=dsl: e.activation(
                gkf3[:, dsl, :], bpr3[:, dsl, :], AF.Exp, bias=LNS,
                scale=-1.0 / 16.0),
                reads=("bpr",), writes=("gkf",))
        for dsl in range(4):
            bank = dsl
            for kc in range(16):
                P.c("tensor", lambda e, dsl=dsl, kc=kc, bank=bank,
                    xnf3=xnf3: e.matmul(
                    ps[bank][:, :], wqs3[:, kc, 128 * dsl:128 * (dsl + 1)],
                    xnf3[:, kc, :], start=(kc == 0), stop=(kc == 15)),
                    reads=("wqs", xr), writes=(psr[bank],))
            P.c("vector", lambda e, dsl=dsl, bank=bank, qe3=qe3:
                e.tensor_tensor(
                    qe3[:, dsl, :], ps[bank][:, :], gkf3[:, dsl, :], ALU.mult),
                reads=(psr[bank], "gkf"), writes=(qr,))
        dq_step()
        for dsl in range(4):
            bank = dsl
            for kc in range(16):
                P.c("tensor", lambda e, dsl=dsl, kc=kc, bank=bank,
                    xnf3=xnf3: e.matmul(
                    ps[bank][:, :], wks3[:, kc, 128 * dsl:128 * (dsl + 1)],
                    xnf3[:, kc, :], start=(kc == 0), stop=(kc == 15)),
                    reads=("wks", xr), writes=(psr[bank],))
            P.c("vector", lambda e, dsl=dsl, bank=bank, kd3=kd3:
                e.tensor_tensor(
                    kd3[:, dsl, :], ps[bank][:, :], enb3[:, dsl, :], ALU.mult),
                reads=(psr[bank], "enb"), writes=(kr_,))
        dq_step()
        for tb in range(4):
            bank = tb % 2
            for kc in range(16):
                P.c("tensor", lambda e, tb=tb, kc=kc, bank=bank,
                    xnf3=xnf3: e.matmul(
                    ps[bank][:, 0:DVH],
                    xnf3[:, kc, 128 * tb:128 * (tb + 1)],
                    wvs3[:, kc, :], start=(kc == 0), stop=(kc == 15)),
                    reads=(xr, "wvs"), writes=(psr[bank],))
            P.c("scalar", lambda e, tb=tb, bank=bank, vt3=vt3: e.copy(
                vt3[:, tb, :], ps[bank][:, 0:DVH]),
                reads=(psr[bank],), writes=(vr,))
        dq_step()
        for tb in range(4):
            bank = 2 + tb % 2
            for kc in range(16):
                P.c("tensor", lambda e, tb=tb, kc=kc, bank=bank,
                    xnf3=xnf3: e.matmul(
                    ps[bank][:, 0:DVH],
                    xnf3[:, kc, 128 * tb:128 * (tb + 1)],
                    wgs3[:, kc, :], start=(kc == 0), stop=(kc == 15)),
                    reads=(xr, "wgs"), writes=(psr[bank],))
            P.c("scalar", lambda e, tb=tb, bank=bank: e.copy(
                gt3[:, tb, :], ps[bank][:, 0:DVH]),
                reads=(psr[bank],), writes=("gt",))
        P.d("scalar", lambda e, h=h, t=t: e.dma_start(
            out=scr_g[t, :, :, DVH * h:DVH * (h + 1)]
                .rearrange("b p f -> p b f"),
            in_=gt3[:, :, :]),
            reads=("gt",), writes=("scr_g",))
        P.d("scalar", lambda e, h=h, t=t, par=par: e.dma_start(
            out=scr_qe[h, t, :, :], in_=qes[par][:, :]),
            reads=(qr,), writes=("scr_qe",))

    def emit_scan(h, t):
        par = (h * NTA + t) % 2
        qr, kr_, vr, er = f"qe{par}", f"kd{par}", f"vt{par}", f"ebA{par}"
        qe3, kd3, vt3 = qe3s[par], kd3s[par], vt3s[par]
        eblAv = eblAs[par]
        if t == 0:
            P.c("vector", lambda e: e.memset(S0[:, :], 0.0), writes=("S0",))
            P.c("vector", lambda e: e.memset(Sbuf[:, :], 0.0), writes=("Sbuf",))
        for c in range(2):
            gc = t * 2 + c
            Sold = S03 if gc % 2 == 0 else S13
            Snew = S13 if gc % 2 == 0 else S03
            Sold_r = "S0" if gc % 2 == 0 else "S1"
            Snew_r = "S1" if gc % 2 == 0 else "S0"
            for sbk in range(2):
                bank = 4 + sbk
                for dsl in range(4):
                    P.c("tensor", lambda e, c=c, sbk=sbk, dsl=dsl, bank=bank,
                        kd3=kd3: e.transpose(
                            psb[bank][:, 128 * dsl:128 * (dsl + 1)],
                            kd3[:, dsl,
                                CH * c + 128 * sbk:CH * c + 128 * (sbk + 1)],
                            idb[:, :]),
                        reads=(kr_, "idb"), writes=(psr[bank],))
                P.c("vector", lambda e, sbk=sbk, bank=bank: e.tensor_copy(
                    kdt3[:, sbk, :],
                    psb[bank][:, 0:512].rearrange("p (b f) -> p b f", f=128)),
                    reads=(psr[bank],), writes=("kdt",))
            for sbk in range(2):
                bank = 4 + sbk
                ncols = CH if sbk == 0 else 128
                coff = CH * c if sbk == 0 else CH * c + 128
                for dsl in range(4):
                    P.c("tensor",
                        lambda e, c=c, sbk=sbk, dsl=dsl, bank=bank,
                        ncols=ncols, coff=coff, kd3=kd3, qe3=qe3: e.matmul(
                            ps[bank][:, 0:ncols],
                            kd3[:, dsl,
                                CH * c + 128 * sbk:CH * c + 128 * (sbk + 1)],
                            qe3[:, dsl, coff:coff + ncols],
                            start=(dsl == 0), stop=(dsl == 3)),
                        reads=(kr_, qr), writes=(psr[bank],))
            P.c("vector", lambda e: e.tensor_tensor(
                Ab3[:, 0, :], ps[4][:, 0:CH], maskm[:, :], ALU.mult),
                reads=(psr[4], "maskm"), writes=("Ab",))
            P.c("vector", lambda e: e.tensor_tensor(
                Ab3[:, 1, 128:CH], ps[5][:, 0:128], maskm[:, 0:128],
                ALU.mult),
                reads=(psr[5], "maskm"), writes=("Ab",))
            for cb in range(2):
                bank = 6 + cb
                mm = []
                if cb == 0:
                    mm.append((0, 0, 2 * c))
                else:
                    mm.append((0, 128, 2 * c))
                    mm.append((1, 128, 2 * c + 1))
                nmm = len(mm) + (0 if gc == 0 else 4)
                idx = 0
                for (sbk, aoff, vslot) in mm:
                    P.c("tensor",
                        lambda e, sbk=sbk, aoff=aoff, vslot=vslot,
                        bank=bank, idx=idx, nmm=nmm, vt3=vt3: e.matmul(
                            ps[bank][:, 0:DVH],
                            Ab3[:, sbk, aoff:aoff + 128],
                            vt3[:, vslot, :],
                            start=(idx == 0), stop=(idx == nmm - 1)),
                        reads=("Ab", vr), writes=(psr[bank],))
                    idx += 1
                if gc != 0:
                    for dsl in range(4):
                        P.c("tensor",
                            lambda e, c=c, cb=cb, dsl=dsl, bank=bank,
                            idx=idx, nmm=nmm, qe3=qe3: e.matmul(
                                ps[bank][:, 0:DVH],
                                qe3[:, dsl,
                                    CH * c + 128 * cb:CH * c + 128 * (cb + 1)],
                                Sb3[:, dsl, :],
                                start=(idx == 0), stop=(idx == nmm - 1)),
                            reads=(qr, "Sbuf"), writes=(psr[bank],))
                        idx += 1
                P.c("scalar", lambda e, c=c, cb=cb, bank=bank: e.copy(
                    ot3[:, 2 * c + cb, :], ps[bank][:, 0:DVH]),
                    reads=(psr[bank],), writes=("ot",))
            for dsl in range(4):
                bank = 4 + dsl % 2
                for sbk in range(2):
                    P.c("tensor",
                        lambda e, dsl=dsl, sbk=sbk, bank=bank, c=c,
                        vt3=vt3: e.matmul(
                            ps[bank][:, 0:DVH],
                            kdt3[:, sbk, 128 * dsl:128 * (dsl + 1)],
                            vt3[:, 2 * c + sbk, :],
                            start=(sbk == 0), stop=(sbk == 1)),
                        reads=("kdt", vr), writes=(psr[bank],))
                P.c("vector", lambda e, dsl=dsl, bank=bank, Snew=Snew,
                    Sold=Sold: e.tensor_tensor(
                        Snew[:, dsl, :], Sold[:, dsl, :], ps[bank][:, 0:DVH],
                        ALU.add),
                    reads=(Sold_r, psr[bank]), writes=(Snew_r,))
                P.c("vector", lambda e, dsl=dsl, Snew=Snew, c=c,
                    eblAv=eblAv: e.tensor_scalar(
                        Snew[:, dsl, :], Snew[:, dsl, :],
                        eblAv[:, 2 * dsl + c:2 * dsl + c + 1], None, ALU.mult),
                    reads=(Snew_r, er), writes=(Snew_r,))
                P.c("vector", lambda e, dsl=dsl, Snew=Snew: e.tensor_copy(
                    Sb3[:, dsl, :], Snew[:, dsl, :]),
                    reads=(Snew_r,), writes=("Sbuf",))
        P.d("scalar", lambda e, h=h, t=t: e.dma_start(
            out=scr_o[t, :, :, DVH * h:DVH * (h + 1)]
                .rearrange("b p f -> p b f"),
            in_=ot3[:, :, :]),
            reads=("ot",), writes=("scr_o",))
        if t == NTA - 1:
            Sfin = S03 if (NCH - 1) % 2 == 1 else S13
            Sfin_r = "S0" if (NCH - 1) % 2 == 1 else "S1"
            P.d("scalar", lambda e, h=h, Sfin=Sfin: e.dma_start(
                out=st_loc[512 * h:512 * (h + 1), :]
                    .rearrange("(d p) f -> p d f", p=128),
                in_=Sfin[:, :, :]),
                reads=(Sfin_r,), writes=("st_loc",))
            P.cc(lambda e, h=h: e.collective_compute(
                "AllGather", ALU.bypass, replica_groups=[list(range(NCORE))],
                ins=[st_loc[512 * h:512 * (h + 1), :].opt()],
                outs=[st_all_h[h][:, :].opt()]),
                reads=("st_loc",), writes=(f"st_all{h}",), tag=f"ag_st{h}")

    items = [(h, t) for h in range(H) for t in range(NTA)]

    def emit_xnf_load(i):
        t = items[i][1]
        par = i % 2
        xr = f"xnf{par}"
        P.d("sync", lambda e, t=t, par=par: e.dma_start(
            out=xnfs[par][:, :], in_=scr_xn[t, :, :]),
            reads=(xr, "scr_xn"), writes=(xr,))

    dq_step()
    emit_xnf_load(0)
    for i, (h, t) in enumerate(items):
        if t == 0:
            load_head_weights(h)
        if i + 1 < len(items):
            emit_xnf_load(i + 1)
        emit_proj(h, t)
        if i > 0:
            emit_scan(*items[i - 1])
    emit_scan(*items[-1])
    while dq_pos[0] < len(dq_units):
        dq_step()

    P.fence()

    P.c("vector", lambda e: e.memset(Sp[:, :], 0.0),
        writes=("Sp0", "Sp1", "Sp2", "Sp3"))
    for h in range(H):
        for r in range(NCORE):
            sl = (h * NCORE + r) % 2
            stb = st_tmp3 if sl == 0 else st_tmp23
            sres = "hfm" if sl == 0 else "ynfm"
            P.d("sync", lambda e, h=h, r=r, stb=stb: e.dma_start(
                out=stb[:, :, :],
                in_=st_all_h[h][512 * r:512 * (r + 1), :]
                    .rearrange("(g p) f -> p g f", p=128)),
                reads=(f"st_all{h}", sres), writes=(sres,))
            P.c("vector", lambda e, h=h, r=r, stb=stb: e.scalar_tensor_tensor(
                Sp3[:, 4 * h:4 * (h + 1), :], stb[:, :, :],
                pmask[:, r:r + 1], Sp3[:, 4 * h:4 * (h + 1), :],
                ALU.mult, ALU.add),
                reads=(sres, "pmask", f"Sp{h}"), writes=(f"Sp{h}",))
    P.c("scalar", lambda e: e.activation(
        eblB[:, :], btot[:, :], AF.Exp, bias=0.0, scale=-1.0 / 16.0),
        reads=("btot",), writes=("eblB",))

    def emit_tile_loads(t):
        P.d("sync", lambda e, t=t: e.dma_start(
            out=ob3[:, :, :],
            in_=scr_o[t, :, :, :].rearrange("b p f -> p b f")),
            reads=("obuf", "scr_o"), writes=("obuf",))
        P.d("sync", lambda e, t=t: e.dma_start(
            out=gb3[:, :, :],
            in_=scr_g[t, :, :, :].rearrange("b p f -> p b f")),
            reads=("gbuf", "scr_g"), writes=("gbuf",))
    def emit_qeb_load(t, hb):
        for h in range(H):
            P.d("sync", lambda e, t=t, h=h, hb=hb: e.dma_start(
                out=qeb3[:, 4 * h:4 * (h + 1), :],
                in_=scr_qe[h, t, :, :]
                    .rearrange("p (d f) -> p d f", f=TA)
                    [:, :, CH * hb:CH * (hb + 1)]),
                reads=("qebuf", "scr_qe"), writes=("qebuf",))

    emit_tile_loads(0)
    emit_qeb_load(0, 0)
    def mk_ln4(t):
        def th():
            for tb in range(4):
                for i in range(2):
                    P.c("vector", lambda e, tb=tb, i=i: e.bn_stats(
                        statS[5][:, 6 * i:6 * (i + 1)],
                        ffn043[:, tb, 512 * i:512 * (i + 1)]),
                        reads=("ynfm",), writes=("stat5",))
                    P.c("vector", lambda e, tb=tb, i=i: e.bn_stats(
                        statS[5][:, 12 + 6 * i:12 + 6 * (i + 1)],
                        ffn143[:, tb, 512 * i:512 * (i + 1)]),
                        reads=("hfm",), writes=("stat5",))
                P.c("vector", lambda e: e.bn_aggr(
                    mvS[5][:, :], statS[5][:, 0:24]),
                    reads=("stat5",), writes=("stat5",))
                P.c("scalar", lambda e: e.activation(
                    t1S[5][:, :], mvS[5][:, 1:2], AF.Ln, bias=EPS,
                    scale=1.0),
                    reads=("stat5",), writes=("rb5",))
                P.c("scalar", lambda e: e.activation(
                    rS[5][:, :], t1S[5][:, :], AF.Exp, bias=0.0,
                    scale=-0.5),
                    reads=("rb5",), writes=("rb5",))
                P.c("vector", lambda e: e.tensor_scalar(
                    bS[5][:, :], mvS[5][:, 0:1], rS[5][:, :], -1.0,
                    ALU.mult, ALU.mult),
                    reads=("stat5", "rb5"), writes=("rb5",))
                P.c("scalar", lambda e, tb=tb: e.activation(
                    ffn043[:, tb, :], ffn043[:, tb, :], AF.Identity,
                    bias=bS[5][:, :], scale=rS[5][:, :]),
                    reads=("ynfm", "rb5"), writes=("ynfm",))
                P.c("vector", lambda e, tb=tb: e.tensor_tensor(
                    ffn043[:, tb, :], ffn043[:, tb, :],
                    xyb43[:, tb, 0:1024], ALU.add),
                    reads=("ynfm", "hfm"), writes=("ynfm",))
                P.d("scalar", lambda e, t=t, tb=tb: e.dma_start(
                    out=out_hbm[TB * t + 128 * tb:TB * t + 128 * (tb + 1),
                                0:1024],
                    in_=ffn043[:, tb, :]),
                    reads=("ynfm",), writes=("out",))
                P.c("scalar", lambda e, tb=tb: e.activation(
                    ffn143[:, tb, :], ffn143[:, tb, :], AF.Identity,
                    bias=bS[5][:, :], scale=rS[5][:, :]),
                    reads=("hfm", "rb5"), writes=("hfm",))
                P.c("vector", lambda e, tb=tb: e.tensor_tensor(
                    ffn143[:, tb, :], ffn143[:, tb, :],
                    xyb43[:, tb, 1024:2048], ALU.add),
                    reads=("hfm",), writes=("hfm",))
                P.d("scalar", lambda e, t=t, tb=tb: e.dma_start(
                    out=out_hbm[TB * t + 128 * tb:TB * t + 128 * (tb + 1),
                                1024:2048],
                    in_=ffn143[:, tb, :]),
                    reads=("hfm",), writes=("out",))
        return th

    pend_ln4 = [None]

    for t in range(NTB):
        P.c("scalar", lambda e: e.activation(gbuf[:, :], gbuf[:, :], AF.Silu),
            reads=("gbuf",), writes=("gbuf",))
        P.c("vector", lambda e: e.tensor_tensor(
            gb3[:, :, :], gb3[:, :, :],
            wnormbc[:, :].unsqueeze(1).broadcast_to([128, 4, VD]),
            ALU.mult),
            reads=("gbuf", "wnormbc"), writes=("gbuf",))
        for hb in range(2):
            gchunk = 2 * t + hb
            for h in range(H):
                for dsl in range(4):
                    P.c("vector", lambda e, h=h, dsl=dsl: e.tensor_copy(
                        Spb3[:, dsl, :], Sp3[:, 4 * h + dsl, :]),
                        reads=(f"Sp{h}",), writes=("Spb",))
                for cb in range(2):
                    for dsl in range(4):
                        P.c("tensor", lambda e, h=h, hb=hb, cb=cb, dsl=dsl:
                            e.matmul(
                                ps[4][:, 0:DVH],
                                qeb3[:, 4 * h + dsl,
                                     128 * cb:128 * (cb + 1)],
                                Spb3[:, dsl, :],
                                start=(dsl == 0), stop=(dsl == 3)),
                            reads=("qebuf", "Spb"), writes=(psr[4],))
                    P.c("vector", lambda e, h=h, hb=hb, cb=cb: e.tensor_tensor(
                        ob3[:, 2 * hb + cb, DVH * h:DVH * (h + 1)],
                        ob3[:, 2 * hb + cb, DVH * h:DVH * (h + 1)],
                        ps[4][:, 0:DVH], ALU.add),
                        reads=("obuf", psr[4]), writes=("obuf",))
            if hb == 0:
                emit_qeb_load(t, 1)
            elif t + 1 < NTB:
                emit_qeb_load(t + 1, 0)
            rmsE, rmsR = rmsEs[hb], rmsRs[hb]
            rres = f"rms{hb}"
            for blk in range(2):
                for h in range(4):
                    i = 4 * blk + h
                    P.c("vector", lambda e, blk=blk, h=h, i=i, hb=hb:
                        e.bn_stats(
                            statS[4][:, 6 * i:6 * (i + 1)],
                            ob3[:, 2 * hb + blk, DVH * h:DVH * (h + 1)]),
                        reads=("obuf",), writes=("stat4",))
                for h in range(4):
                    i = 4 * blk + h
                    P.c("vector", lambda e, i=i: e.bn_aggr(
                        mvS[4][:, :], statS[4][:, 6 * i:6 * (i + 1)]),
                        reads=("stat4",), writes=("stat4m",))
                    P.c("vector", lambda e, i=i, rmsE=rmsE:
                        e.scalar_tensor_tensor(
                            rmsE[:, i:i + 1], mvS[4][:, 0:1], mvS[4][:, 0:1],
                            mvS[4][:, 1:2], ALU.mult, ALU.add),
                        reads=("stat4m",), writes=(rres,))
                P.c("scalar", lambda e, blk=blk, rmsE=rmsE, rmsR=rmsR:
                    e.activation(
                        rmsR[:, 4 * blk:4 * (blk + 1)],
                        rmsE[:, 4 * blk:4 * (blk + 1)], AF.Ln, bias=EPS,
                        scale=1.0),
                    reads=(rres,), writes=(rres,))
                P.c("scalar", lambda e, blk=blk, rmsR=rmsR: e.activation(
                    rmsR[:, 4 * blk:4 * (blk + 1)],
                    rmsR[:, 4 * blk:4 * (blk + 1)], AF.Exp, bias=0.0,
                    scale=-0.5),
                    reads=(rres,), writes=(rres,))
                for h in range(4):
                    i = 4 * blk + h
                    P.c("vector", lambda e, blk=blk, h=h, i=i, hb=hb,
                        rmsR=rmsR: e.scalar_tensor_tensor(
                            oftm3[:, blk, DVH * h:DVH * (h + 1)],
                            ob3[:, 2 * hb + blk, DVH * h:DVH * (h + 1)],
                            rmsR[:, i:i + 1],
                            gb3[:, 2 * hb + blk, DVH * h:DVH * (h + 1)],
                            ALU.mult, ALU.mult),
                        reads=("obuf", rres, "gbuf"), writes=("mixA",))
                coff = CH * hb + 128 * blk
                for half in range(2):
                    bank = 6 + half
                    for i in range(4):
                        uc = 4 * half + i
                        P.c("tensor", lambda e, blk=blk, uc=uc, i=i, bank=bank:
                            e.transpose(
                                psb[bank][:, 128 * i:128 * (i + 1)],
                                oftm3[:, blk, 128 * uc:128 * (uc + 1)],
                                idb[:, :]),
                            reads=("mixA", "idb"), writes=(psr[bank],))
                    P.c("vector", lambda e, half=half, coff=coff, bank=bank:
                        e.tensor_copy(
                            offm3[:, 4 * half:4 * (half + 1), coff:coff + 128],
                            psb[bank][:, 0:512]
                            .rearrange("p (b f) -> p b f", f=128)),
                        reads=(psr[bank],), writes=("offm",))
            for h in range(H):
                for dsl in range(4):
                    col = h * 64 + dsl * 16 + gchunk
                    P.c("vector", lambda e, h=h, dsl=dsl, col=col:
                        e.tensor_scalar(
                            Sp3[:, 4 * h + dsl, :], Sp3[:, 4 * h + dsl, :],
                            eblB[:, col:col + 1], None, ALU.mult),
                        reads=(f"Sp{h}", "eblB"), writes=(f"Sp{h}",))
        if pend_ln4[0] is not None:
            pend_ln4[0]()
            pend_ln4[0] = None
        if t + 1 < NTB:
            emit_tile_loads(t + 1)
        for pair in range(2):
            for uc in range(8):
                slot = uc % 2
                P.d("sync", lambda e, uc=uc, slot=slot: e.dma_start(
                    out=wob3[:, slot, :],
                    in_=scr_wo[128 * uc:128 * (uc + 1), :]),
                    reads=(f"wob{slot}", "scr_wo"), writes=(f"wob{slot}",))
                for tb2 in range(2):
                    tb = 2 * pair + tb2
                    for eg in range(4):
                        bank = 4 * tb2 + eg
                        P.c("tensor",
                            lambda e, uc=uc, slot=slot, tb=tb, eg=eg, bank=bank:
                            e.matmul(
                                ps[bank][:, :],
                                offm3[:, uc, 128 * tb:128 * (tb + 1)],
                                wob3[:, slot, 512 * eg:512 * (eg + 1)],
                                start=(uc == 0), stop=(uc == 7)),
                            reads=("offm", f"wob{slot}"), writes=(psr[bank],))
            for tb2 in range(2):
                tb = 2 * pair + tb2
                P.d("gpsimd", lambda e, t=t, tb=tb: e.dma_start(
                    out=xyb[:, :],
                    in_=x_in[TB * t + 128 * tb:TB * t + 128 * (tb + 1), :]),
                    reads=("xyb",), writes=("xyb",))
                rB, bB, rres = ln_stats(
                    lambda i, tb2=tb2: ps[4 * tb2 + i][:, :], 4,
                    (psr[4 * tb2], psr[4 * tb2 + 1], psr[4 * tb2 + 2],
                     psr[4 * tb2 + 3]), si=tb2)
                for eg in range(4):
                    bank = 4 * tb2 + eg
                    P.c("vector", lambda e, eg=eg, bank=bank, rB=rB:
                        e.scalar_tensor_tensor(
                            xyb[:, 512 * eg:512 * (eg + 1)],
                            ps[bank][:, :], rB[:, :],
                            xyb[:, 512 * eg:512 * (eg + 1)],
                            ALU.mult, ALU.add),
                        reads=(psr[bank], rres, "xyb"), writes=("xyb",))
                P.c("vector", lambda e, bB=bB: e.tensor_scalar(
                    xyb[:, :], xyb[:, :], bB[:, :], None, ALU.add),
                    reads=("xyb", rres), writes=("xyb",))
                P.d("scalar", lambda e, t=t, tb=tb: e.dma_start(
                    out=scr_y[t, tb, :, :], in_=xyb[:, :]),
                    reads=("xyb",), writes=("scr_y",))
                rB3, bB3, rres3 = ln_stats(
                    lambda i: xyb[:, 512 * i:512 * (i + 1)], 4, ("xyb",),
                    si=2 + tb2)
                for rnd in range(4):
                    P.c("scalar", lambda e, rnd=rnd, rB3=rB3, bB3=bB3:
                        e.activation(
                            yntm[:, 512 * rnd:512 * (rnd + 1)],
                            xyb[:, 512 * rnd:512 * (rnd + 1)], AF.Identity,
                            bias=bB3[:, :], scale=rB3[:, :]),
                        reads=("xyb", rres3), writes=("mixB",))
                    bank = 4 * tb2 + rnd % 2
                    for i in range(4):
                        P.c("tensor", lambda e, rnd=rnd, i=i, bank=bank:
                            e.transpose(
                                psb[bank][:, 128 * i:128 * (i + 1)],
                                yntm[:, 512 * rnd + 128 * i:
                                     512 * rnd + 128 * (i + 1)],
                                idb[:, :]),
                            reads=("mixB", "idb"), writes=(psr[bank],))
                    P.c("vector", lambda e, rnd=rnd, tb=tb, bank=bank:
                        e.tensor_copy(
                            ynfm3[:, 4 * rnd:4 * (rnd + 1),
                                  128 * tb:128 * (tb + 1)],
                            psb[bank][:, 0:512]
                            .rearrange("p (b f) -> p b f", f=128)),
                        reads=(psr[bank],), writes=("ynfm",))
        wup_all4 = wup_all.rearrange("(k m p) f -> k m p f", m=16, p=128)
        for mg in range(16):
            pb = 4 * (mg % 2)
            for jj in range(4):
                slot = jj % 2
                P.d("sync", lambda e, mg=mg, jj=jj, slot=slot: e.dma_start(
                    out=upb4[:, slot, :, :],
                    in_=wup_all4[4 * jj:4 * (jj + 1), mg, :, :]
                        .rearrange("k p f -> p k f")),
                    reads=(f"upb{slot}", "wup_all"), writes=(f"upb{slot}",))
                for kk in range(4):
                    k = 4 * jj + kk
                    for mb in range(4):
                        P.c("tensor",
                            lambda e, slot=slot, kk=kk, k=k, mb=mb, pb=pb:
                            e.matmul(
                                ps[pb + mb][:, :],
                                upb4[:, slot, kk, 128 * mb:128 * (mb + 1)],
                                ynfm3[:, k, :],
                                start=(k == 0), stop=(k == 15)),
                            reads=(f"upb{slot}", "ynfm"),
                            writes=(psr[pb + mb],))
            for mb in range(4):
                m = 4 * mg + mb
                bank = pb + mb
                P.c("scalar", lambda e, m=m, bank=bank: e.activation(
                    glu_s[:, :], ps[bank][:, :], AF.Square,
                    bias=ubf[:, m:m + 1], scale=1.0),
                    reads=(psr[bank], "ubf"), writes=("mixA",))
                P.c("vector", lambda e: e.tensor_scalar(
                    glu_u[:, :], glu_s[:, :], GC2, GC1, ALU.mult, ALU.add),
                    reads=("mixA",), writes=("mixA",))
                P.c("vector", lambda e, m=m, bank=bank: e.scalar_tensor_tensor(
                    glu_z[:, :], ps[bank][:, :], ubf[:, m:m + 1], glu_u[:, :],
                    ALU.add, ALU.mult),
                    reads=(psr[bank], "ubf", "mixA"), writes=("mixB",))
                P.c("scalar", lambda e: e.activation(
                    glu_s[:, :], glu_z[:, :], AF.Sigmoid),
                    reads=("mixB",), writes=("mixA",))
                P.c("vector", lambda e, m=m, bank=bank: e.scalar_tensor_tensor(
                    hfm3[:, m, :], ps[bank][:, :], ubf[:, m:m + 1],
                    glu_s[:, :], ALU.add, ALU.mult),
                    reads=(psr[bank], "ubf", "mixA"), writes=("hfm",))
        wdn_all4 = wdn_all.rearrange("(k h p) f -> k h p f", h=2, p=128)
        for eh in range(2):
            for jj in range(32):
                slot = jj % 2
                P.d("sync", lambda e, eh=eh, jj=jj, slot=slot: e.dma_start(
                    out=dnb4[:, slot, :, :],
                    in_=wdn_all4[2 * jj:2 * (jj + 1), eh, :, :]
                        .rearrange("k p f -> p k f")),
                    reads=(f"dnb{slot}", "wdn_all"), writes=(f"dnb{slot}",))
                for kk in range(2):
                    k = 2 * jj + kk
                    for tb in range(4):
                        for half in range(2):
                            bank = 2 * tb + half
                            P.c("tensor",
                                lambda e, slot=slot, kk=kk, k=k, tb=tb,
                                half=half, bank=bank: e.matmul(
                                    ps[bank][:, :],
                                    hfm3[:, k, 128 * tb:128 * (tb + 1)],
                                    dnb4[:, slot, kk,
                                         512 * half:512 * (half + 1)],
                                    start=(k == 0), stop=(k == 63)),
                                reads=(f"dnb{slot}", "hfm"),
                                writes=(psr[bank],))
            if eh == 0:
                for tb in range(4):
                    for half in range(2):
                        bank = 2 * tb + half
                        P.c("vector", lambda e, tb=tb, half=half, bank=bank:
                            e.tensor_tensor(
                                ffn043[:, tb, 512 * half:512 * (half + 1)],
                                ps[bank][:, :],
                                dbbc[:, 512 * half:512 * (half + 1)], ALU.add),
                            reads=(psr[bank], "dbbc"),
                            writes=("ynfm",))
            else:
                for tb in (2, 3, 0, 1):
                    for half in range(2):
                        bank = 2 * tb + half
                        P.c("vector", lambda e, tb=tb, half=half, bank=bank:
                            e.tensor_tensor(
                                ffn143[:, tb, 512 * half:512 * (half + 1)],
                                ps[bank][:, :],
                                dbbc[:, 1024 + 512 * half:
                                     1024 + 512 * (half + 1)], ALU.add),
                            reads=(psr[bank], "dbbc"),
                            writes=("hfm",))
                for tb in range(4):
                    P.d("gpsimd", lambda e, t=t, tb=tb: e.dma_start(
                        out=xyb43[:, tb, :], in_=scr_y[t, tb, :, :]),
                        reads=("scr_y",), writes=("hfm",))
                pend_ln4[0] = mk_ln4(t)

    if pend_ln4[0] is not None:
        pend_ln4[0]()
        pend_ln4[0] = None

    if DEBUG_TAPS:
        taps = [("t_xn", scr_xn, "scr_xn"), ("t_o", scr_o, "scr_o"),
                ("t_g", scr_g, "scr_g"), ("t_qe", scr_qe, "scr_qe"),
                ("t_y", scr_y, "scr_y"), ("t_st", st_loc, "st_loc"),
                ("t_wup", wup_loc, "wloc"), ("t_wdn", wdn_loc, "wloc")]
        for name, ap, res in taps:
            shp = list(ap.shape)
            tout = nc.dram_tensor(name, shp, ap.dtype, kind="ExternalOutput").ap()
            P.d("gpsimd", lambda e, tt=tout, a=ap: e.dma_start(out=tt, in_=a),
                reads=(res,), writes=())
        btout = nc.dram_tensor("t_bt", [128, H * 4 * NCH], F32,
                               kind="ExternalOutput").ap()
        P.d("gpsimd", lambda e: e.dma_start(out=btout, in_=btot[:, :]),
            reads=("btot",), writes=())

    P.finalize()
    fc = P.final_counts
    hw = {}
    with nc.Block() as block:

        @block.tensor
        def _(eng):
            P.replay("tensor", eng, sems, hw)

        @block.vector
        def _(eng):
            P.replay("vector", eng, sems, hw)

        @block.scalar
        def _(eng):
            P.replay("scalar", eng, sems, hw)

        @block.gpsimd
        def _(eng):
            P.replay("gpsimd", eng, sems, hw)

        @block.sync
        def _(eng):
            P.replay("sync", eng, sems, hw)
            for key, v in sorted(fc.items()):
                if key[0] in ("dma", "cc"):
                    eng.wait_ge(sems[key], v)

    nc._keepalive_stack = stack
    return nc


_CACHE = {}
_LAST_RES = None


def kernel(**inputs):
    x = np.asarray(inputs["L0_attn_x"], dtype=np.float32)
    wq = np.asarray(inputs["L0_attn_Wq"], dtype=np.float32)
    wk = np.asarray(inputs["L0_attn_Wk"], dtype=np.float32)
    wv = np.asarray(inputs["L0_attn_Wv"], dtype=np.float32)
    wgk1 = np.asarray(inputs["L0_attn_Wgk1"], dtype=np.float32)
    wgk2 = np.asarray(inputs["L0_attn_Wgk2"], dtype=np.float32)
    bgk2 = np.asarray(inputs["L0_attn_bgk2"], dtype=np.float32)
    wg = np.asarray(inputs["L0_attn_Wg"], dtype=np.float32)
    wnorm = np.asarray(inputs["L0_attn_Wnorm"], dtype=np.float32)
    wo = np.asarray(inputs["L0_attn_Wo"], dtype=np.float32)
    wup = np.asarray(inputs["L0_ffn_up_kernel"], dtype=np.float32)
    ub = np.asarray(inputs["L0_ffn_up_bias"], dtype=np.float32)
    wdn = np.asarray(inputs["L0_ffn_down_kernel"], dtype=np.float32)
    db = np.asarray(inputs["L0_ffn_down_bias"], dtype=np.float32)

    xf = np.ascontiguousarray(x.reshape(NCORE * TPC, D))
    bgk2f = np.ascontiguousarray(bgk2.reshape(16, 128).T)
    ubf = np.ascontiguousarray(ub.reshape(64, 128).T)
    dbbc = np.ascontiguousarray(np.broadcast_to(db[None, :], (128, D)))
    wnormbc = np.ascontiguousarray(np.tile(wnorm, (128, H)))
    idf = np.eye(128, dtype=np.float32)
    idb = np.eye(128, dtype=ml_dtypes.bfloat16)
    maskmv = (np.arange(128)[:, None] <= np.arange(CH)[None, :]).astype(np.float32)
    cvals = np.array([0.0, 1.0, EPS, LNS, -1.0, -0.5, 1.0 / 16.0, -1.0 / 16.0],
                     dtype=np.float32)
    cpoolv = np.ascontiguousarray(np.broadcast_to(cvals[None, :], (128, 8)))

    ck = ("nc", DEBUG_TAPS)
    if ck not in _CACHE:
        _CACHE[ck] = build_program()
    nc = _CACHE[ck]

    in_maps = []
    for c in range(NCORE):
        pm = np.zeros((128, NCORE), dtype=np.float32)
        if c % 2 == 1:
            pm[:, c - 1] = 1.0
        in_maps.append({
            "x": np.ascontiguousarray(xf[c * TPC:(c + 1) * TPC]),
            "wq": wq, "wk": wk, "wv": wv, "wgk1": wgk1, "wgk2": wgk2,
            "bgk2f": bgk2f, "wg": wg, "wnormbc": wnormbc, "wo": wo,
            "wupsl": np.ascontiguousarray(wup[c * 256:(c + 1) * 256]),
            "ubf": ubf,
            "wdnsl": np.ascontiguousarray(wdn[c * 1024:(c + 1) * 1024]),
            "dbbc": dbbc, "pmask": pm, "idf": idf, "idb": idb,
            "maskm": maskmv, "cpool": cpoolv,
        })
    res = run_bass_kernel_spmd(nc, in_maps, core_ids=list(range(NCORE)))
    global _LAST_RES
    _LAST_RES = res
    outs = [res.results[c]["out"] for c in range(NCORE)]
    return np.concatenate(outs, axis=0).reshape(4, 8192, D)
